# version 22
# baseline (speedup 1.0000x reference)
import numpy as np
import ml_dtypes
from contextlib import ExitStack

import concourse.bass as bass
import concourse.mybir as mybir
from concourse.bass_utils import run_bass_kernel_spmd

F32 = mybir.dt.float32
F32R = mybir.dt.float32r
I32 = mybir.dt.int32
OP = mybir.AluOpType
AF = mybir.ActivationFunctionType
AX = mybir.AxisListType

D = 2048
S = 2048
SEG = 512
NSEG = 4
H = 16
INTER = 5504
NOC = INTER // 128
SCALE = 1.0 / float(np.sqrt(128.0))
EPS = 1e-5
GT = 512

ENGS = ("sync", "tensor", "vector", "scalar", "gpsimd")


class Sem:
    __slots__ = ("name", "h", "n")

    def __init__(self, name):
        self.name = name
        self.h = None
        self.n = 0


class Res:
    __slots__ = ("w", "rs")

    def __init__(self):
        self.w = None
        self.rs = []


class KB:
    def __init__(self, nc):
        self.nc = nc
        self.steps = []
        self.sems = {}
        self.waited = {e: {} for e in ENGS}
        self.pend = {e: [] for e in ENGS}

    def sem(self, name):
        if name not in self.sems:
            self.sems[name] = Sem(name)
        return self.sems[name]

    def q(self, eng, fn):
        self.steps.append((eng, fn))

    def wait(self, eng, tok):
        if tok is None:
            return
        sem, val = tok
        if val <= 0:
            return
        wd = self.waited[eng]
        if wd.get(sem.name, 0) >= val:
            return
        wd[sem.name] = val
        self.q(eng, lambda e, s=sem, v=val: e.wait_ge(s.h, v))

    def waits(self, eng, toks):
        best = {}
        for t in toks:
            if t is None:
                continue
            s, v = t
            if s.name not in best or best[s.name][1] < v:
                best[s.name] = t
        for t in best.values():
            self.wait(eng, t)

    def ei(self, eng, fn, reads=(), writes=(), sem=None, inc_n=1,
           extra_waits=()):
        toks = []
        for r in reads:
            toks.append(r.w)
        for r in writes:
            toks.append(r.w)
            toks.extend(r.rs)
        toks.extend(extra_waits)
        self.waits(eng, toks)
        tok = None
        if sem is not None:
            sem.n += inc_n
            tok = (sem, sem.n)

            def thunk(e, fn=fn, s=sem, n=inc_n):
                fn(e).then_inc(s.h, n)
            self.q(eng, thunk)
        else:
            self.q(eng, lambda e, fn=fn: fn(e))
        for r in writes:
            r.w = tok
            r.rs = []
        if tok is not None:
            for r in reads:
                r.rs.append(tok)
            for rds, wts in self.pend[eng]:
                for r in rds:
                    r.rs.append(tok)
                for r in wts:
                    if r.w is None:
                        r.w = tok
            self.pend[eng] = []
        else:
            self.pend[eng].append((list(reads), list(writes)))
        return tok

    def flush(self, block):
        for eng in ENGS:
            fns = [f for e, f in self.steps if e == eng]
            if not fns:
                continue

            def section(e, fns=fns):
                for f in fns:
                    f(e)
            getattr(block, eng)(section)


def build_nc(dbg=False):
    nc = bass.Bass("TRN2", target_bir_lowering=False)

    xT = nc.dram_tensor("xT", [D, S], F32R, kind="ExternalInput").ap()
    wq = nc.dram_tensor("wq", [D, D], F32R, kind="ExternalInput").ap()
    wk = nc.dram_tensor("wk", [D, D], F32R, kind="ExternalInput").ap()
    wv = nc.dram_tensor("wv", [D, D], F32R, kind="ExternalInput").ap()
    wo = nc.dram_tensor("wo", [D, D], F32R, kind="ExternalInput").ap()
    gidxT = nc.dram_tensor("gidxT", [D // 8, INTER], I32,
                           kind="ExternalInput").ap()
    uidxT = nc.dram_tensor("uidxT", [D // 8, INTER], I32,
                           kind="ExternalInput").ap()
    didxT = nc.dram_tensor("didxT", [INTER // 8, D], I32,
                           kind="ExternalInput").ap()
    lutb = nc.dram_tensor("lutb", [128, 16], F32, kind="ExternalInput").ap()
    betaTin = nc.dram_tensor("betaTin", [128, H], F32, kind="ExternalInput").ap()
    identin = nc.dram_tensor("identin", [128, 128], F32R,
                             kind="ExternalInput").ap()
    maskin = nc.dram_tensor("maskin", [128, 4 * SEG], mybir.dt.bfloat16,
                            kind="ExternalInput").ap()
    onecolin = nc.dram_tensor("onecolin", [128, 2], F32R,
                              kind="ExternalInput").ap()
    onerowin = nc.dram_tensor("onerowin", [1, 128], F32R,
                              kind="ExternalInput").ap()
    sgnin = nc.dram_tensor("sgnin", [128, 15], F32, kind="ExternalInput").ap()
    memzin = nc.dram_tensor("memzin", [128, H * 132], F32R,
                            kind="ExternalInput").ap()

    outT = nc.dram_tensor("outT", [D, S], F32, kind="ExternalOutput").ap()

    ikind = "ExternalOutput"
    y_hbm = nc.dram_tensor("y_hbm", [D, S], F32R, kind=ikind).ap()
    wg_hbmT = nc.dram_tensor("wg_hbm", [D, INTER], F32R, kind="Internal")
    wu_hbmT = nc.dram_tensor("wu_hbm", [D, INTER], F32R, kind="Internal")
    wd_hbmT = nc.dram_tensor("wd_hbm", [INTER, D], F32R, kind="Internal")
    wg_shT = nc.dram_tensor("wg_sh", [D // 8, INTER], F32R, kind="Internal")
    wu_shT = nc.dram_tensor("wu_sh", [D // 8, INTER], F32R, kind="Internal")
    wd_shT = nc.dram_tensor("wd_sh", [INTER // 8, D], F32R, kind="Internal")
    wg_hbm, wu_hbm, wd_hbm = wg_hbmT.ap(), wu_hbmT.ap(), wd_hbmT.ap()
    wg_sh, wu_sh, wd_sh = wg_shT.ap(), wu_shT.ap(), wd_shT.ap()

    ctx = ExitStack()

    def sb(name, shape, dt):
        return ctx.enter_context(nc.sbuf_tensor(name, shape, dt))

    def psum(name, shape):
        return ctx.enter_context(nc.psum_tensor(name, shape, F32))

    XF = sb("XF", [128, 16 * SEG], F32R)
    ATT = sb("ATT", [128, 16 * SEG], F32R)
    VT = [sb(f"VT{i}", [128, 2048], F32R) for i in range(4)]
    WAB = [sb("WA", [128, 4096], F32R), sb("WB", [128, 4096], F32R)]
    QHs = [sb("QH0", [128, SEG], F32R), sb("QH1", [128, SEG], F32R)]
    KHs = [sb("KH0", [128, SEG], F32R), sb("KH1", [128, SEG], F32R)]
    SQs = [sb("SQ0", [128, SEG], F32R), sb("SQ1", [128, SEG], F32R)]
    SKs = [sb("SK0", [128, SEG], F32R), sb("SK1", [128, SEG], F32R)]
    EXPb = sb("EXPb", [128, 2 * SEG], F32R)
    DLT = EXPb[:, 0:SEG]
    SKT = EXPb[:, SEG:2 * SEG]
    GIXG = sb("GIXG", [128, GT], F32)
    GMSK = sb("GMSK", [128, GT], F32)
    GACG = sb("GACG", [128, GT], F32)
    MEMZ = sb("MEMZ", [128, H * 132], F32R)
    SQR = sb("SQR", [128, SEG], F32R)
    SCR1 = sb("SCR1", [128, SEG], F32)
    SCRB = sb("SCRB", [128, SEG], F32)
    AM = sb("AM", [128, SEG], F32)
    OUTt = sb("OUTt", [128, SEG], F32)
    GIX = [sb("GIXA", [128, GT], I32), sb("GIXB", [128, GT], I32)]
    GS = [sb(f"GS{i}", [128, GT], mybir.dt.bfloat16) for i in range(15)]
    GACB = sb("GACB", [128, GT], F32)
    MSK = sb("MSK", [128, 4 * SEG], mybir.dt.bfloat16)
    IDENT = sb("IDENT", [128, 128], F32R)
    LUT = sb("LUT", [128, 16], F32)
    DIF = sb("DIF", [128, 16], F32)
    DIF2 = sb("DIF2", [128, 16], F32)
    CC = sb("CC", [128, 1], F32)
    BET = sb("BET", [128, H], F32)
    OMB = sb("OMB", [128, H], F32)
    BTI = sb("BTI", [128, H], F32)
    MKC = sb("MKC", [128, 1], F32)
    RCP = sb("RCP", [128, 1], F32)
    ZCOL = sb("ZCOL", [128, 1], F32)
    SGNB = sb("SGNB", [128, 15], F32)
    ONEC = sb("ONEC", [128, 2], F32R)
    ONER = sb("ONER", [1, 128], F32R)
    ROWF = sb("ROWF", [1, 2 * SEG], F32)
    ROWR = sb("ROWR", [1, 2 * SEG], F32R)

    P = [psum(f"P{i}", [128, 512]) for i in range(8)]

    def c32(ap):
        return ap.bitcast(F32)

    kb = KB(nc)
    sDM = kb.sem("sDM")
    sXF = kb.sem("sXF")
    sWA = kb.sem("sWA")
    sWB = kb.sem("sWB")
    sVT = [kb.sem(f"sVT{i}") for i in range(4)]
    sGXA = kb.sem("sGXA")
    sGXB = kb.sem("sGXB")
    sWST = [kb.sem(f"sWST{i}") for i in range(3)]
    sYS = kb.sem("sYS")
    sOU = kb.sem("sOU")
    sT = kb.sem("sT")
    sV = kb.sem("sV")
    sS = kb.sem("sS")

    rXF = [Res() for _ in range(16)]
    rATT = [Res() for _ in range(16)]
    rVT = [Res() for _ in range(4)]
    rWAB = [Res(), Res()]
    rQH = [Res(), Res()]
    rKH = [Res(), Res()]
    rSQ = [Res(), Res()]
    rSK = [Res(), Res()]
    rSKT, rDLT = Res(), Res()
    rEXP = [Res(), Res()]
    rMEMZ = [Res() for _ in range(H)]
    rAQ, rSQR, rSCR1, rSCRB, rAM, rOUT = (Res() for _ in range(6))
    rGIX = [Res(), Res()]
    rGIXG, rGMSK, rGACG = Res(), Res(), Res()
    rGS = [Res() for _ in range(15)]
    rGAC = [rOUT, Res()]
    rP = [Res() for _ in range(8)]
    rROWF, rROWR = Res(), Res()
    rMISC = Res()
    rRCP = Res()
    rY = Res()
    rWGH = [Res() for _ in range(3)]

    def dma(pairs, reads=(), writes=(), sem=None):
        sm = sem if sem is not None else sDM

        def fn(e, sm=sm):
            inst = None
            for dst, src in pairs:
                inst = e.dma_start(out=dst, in_=src)
                inst.then_inc(sm.h, 16)
            return inst

        toks = []
        for r in reads:
            toks.append(r.w)
        for r in writes:
            toks.append(r.w)
            toks.extend(r.rs)
        kb.waits("sync", toks)
        sm.n += 16 * len(pairs)
        tok = (sm, sm.n)
        kb.q("sync", fn)
        for r in writes:
            r.w = tok
            r.rs = []
        for r in reads:
            r.rs.append(tok)
        return tok

    sDG = kb.sem("sDG")

    def gdma(pairs, reads=(), writes=()):
        def fn(e):
            inst = None
            for dst, srcp in pairs:
                inst = e.dma_start(out=dst, in_=srcp)
                inst.then_inc(sDG.h, 16)
            return inst
        toks = []
        for r in reads:
            toks.append(r.w)
        for r in writes:
            toks.append(r.w)
            toks.extend(r.rs)
        kb.waits("gpsimd", toks)
        sDG.n += 16 * len(pairs)
        tok = (sDG, sDG.n)
        kb.q("gpsimd", fn)
        for r in writes:
            r.w = tok
            r.rs = []
        for r in reads:
            r.rs.append(tok)
        return tok

    def mm(bank_ap, rres, lhsT, rhs, start, stop, reads, force_sem=False):
        def fn(e, o=bank_ap, l=lhsT, r=rhs, st=start, sp=stop):
            return e.matmul(o, l, r, start=st, stop=sp)
        return kb.ei("tensor", fn, reads=list(reads), writes=[rres],
                     sem=sT if (stop or force_sem) else None)

    def vop(fn, reads=(), writes=(), sem=sV, extra_waits=()):
        return kb.ei("vector", fn, reads=list(reads), writes=list(writes),
                     sem=sem, extra_waits=extra_waits)

    def sop(fn, reads=(), writes=(), sem=sS, extra_waits=()):
        return kb.ei("scalar", fn, reads=list(reads), writes=list(writes),
                     sem=sem, extra_waits=extra_waits)

    vop(lambda e: e.memset(ZCOL[:], 0.0), writes=[rMISC])
    dma([(MEMZ[:], memzin)], writes=rMEMZ)

    dma([(MSK[:], maskin), (IDENT[:], identin), (LUT[:], lutb),
         (BTI[:], betaTin), (ONEC[:], onecolin), (ONER[:], onerowin),
         (SGNB[:], sgnin)], writes=[rMISC])

    sop(lambda e: e.activation(BET[:], BTI[:], AF.Sigmoid, bias=ZCOL[:]),
        reads=[rMISC], writes=[rMISC])
    vop(lambda e: e.tensor_scalar(OMB[:], BET[:], -1.0, 1.0, OP.mult, OP.add),
        reads=[rMISC], writes=[rMISC])
    vop(lambda e: e.tensor_tensor(DIF[:, 1:16], LUT[:, 1:16], LUT[:, 0:15],
                                  OP.subtract), reads=[rMISC], writes=[rMISC])
    vop(lambda e: e.tensor_scalar(DIF[:, 1:16], DIF[:, 1:16], 0.5, None,
                                  OP.mult), reads=[rMISC], writes=[rMISC])
    vop(lambda e: e.tensor_tensor(DIF2[:, 1:16], DIF[:, 1:16], DIF[:, 1:16],
                                  OP.add), reads=[rMISC], writes=[rMISC])
    vop(lambda e: e.reduce_sum(CC[:], DIF[:, 1:16], axis=AX.X),
        reads=[rMISC], writes=[rMISC])
    vop(lambda e: e.tensor_tensor(CC[:], CC[:], LUT[:, 0:1], OP.add),
        reads=[rMISC], writes=[rMISC])

    wab_i = [0]

    def next_wbuf():
        i = wab_i[0] % 2
        wab_i[0] += 1
        return WAB[i], rWAB[i]

    def ln_and_store(sg, store_y):
        vop(lambda e: e.tensor_scalar(ROWF[0:1, 0:SEG], P[2][0:1, :],
                                      1.0 / 2048.0, None, OP.mult),
            reads=[rP[2]], writes=[rROWF])
        vop(lambda e: e.tensor_scalar(ROWF[0:1, SEG:2 * SEG], P[3][0:1, :],
                                      1.0 / 2048.0, None, OP.mult),
            reads=[rP[3]], writes=[rROWF])
        vop(lambda e: e.tensor_tensor(SCR1[0:1, 0:SEG],
                                      ROWF[0:1, 0:SEG], ROWF[0:1, 0:SEG],
                                      OP.mult), reads=[rROWF], writes=[rSCR1])
        vop(lambda e: e.tensor_tensor(ROWF[0:1, SEG:2 * SEG],
                                      ROWF[0:1, SEG:2 * SEG],
                                      SCR1[0:1, 0:SEG], OP.subtract),
            reads=[rROWF, rSCR1], writes=[rROWF])
        vop(lambda e: e.tensor_scalar(ROWF[0:1, SEG:2 * SEG],
                                      ROWF[0:1, SEG:2 * SEG], EPS, None,
                                      OP.add), reads=[rROWF], writes=[rROWF])
        sop(lambda e: e.activation(ROWF[0:1, SEG:2 * SEG],
                                   ROWF[0:1, SEG:2 * SEG], AF.Ln,
                                   bias=ZCOL[0:1, :]),
            reads=[rROWF, rMISC], writes=[rROWF])
        sop(lambda e: e.activation(ROWR[0:1, SEG:2 * SEG],
                                   ROWF[0:1, SEG:2 * SEG], AF.Exp,
                                   bias=ZCOL[0:1, :], scale=-0.5),
            reads=[rROWF, rMISC], writes=[rROWR])
        vop(lambda e: e.tensor_copy(ROWR[0:1, 0:SEG], ROWF[0:1, 0:SEG]),
            reads=[rROWF], writes=[rROWR])
        mm(P[4][:], rP[4], ONER[:], ROWR[0:1, 0:SEG], True, True,
           reads=[rROWR, rMISC])
        mm(P[5][:], rP[5], ONER[:], ROWR[0:1, SEG:2 * SEG], True, True,
           reads=[rROWR, rMISC])
        sop(lambda e: e.copy(AM[:], P[4][:]), reads=[rP[4]], writes=[rAM])
        sop(lambda e: e.copy(SCRB[:], P[5][:]), reads=[rP[5]], writes=[rSCRB])
        vop(lambda e: e.tensor_tensor(AM[:], AM[:], SCRB[:], OP.mult),
            reads=[rSCRB, rAM], writes=[rAM])
        for dc in range(16):
            vop(lambda e, dc=dc: e.tensor_tensor(
                SCR1[:], c32(XF[:, dc * SEG:(dc + 1) * SEG]), SCRB[:],
                OP.mult),
                reads=[rXF[dc], rSCRB], writes=[rSCR1])
            if store_y:
                vop(lambda e, dc=dc: e.tensor_tensor(
                    XF[:, dc * SEG:(dc + 1) * SEG], SCR1[:], AM[:],
                    OP.subtract), reads=[rSCR1, rAM], writes=[rXF[dc]])
            else:
                vop(lambda e: e.tensor_tensor(OUTt[:], SCR1[:], AM[:],
                                              OP.subtract),
                    reads=[rSCR1, rAM], writes=[rOUT])
                dma([(outT[dc * 128:(dc + 1) * 128, sg * SEG:(sg + 1) * SEG],
                      OUTt[:])], reads=[rOUT], sem=sOU)
        if store_y:
            dst = y_hbm[:, sg * SEG:(sg + 1) * SEG].rearrange(
                "(c p) s -> p c s", p=128)
            dma([(dst, XF[:].rearrange("p (c s) -> p c s", s=SEG))],
                reads=rXF, writes=[rY], sem=sYS)

    def seg_attention(sg):
        t0 = sg * SEG
        xsrc = xT[:, t0:t0 + SEG].rearrange("(c p) s -> p c s", p=128)
        xdst = XF[:].rearrange("p (c s) -> p c s", s=SEG)
        dma([(xdst, xsrc)], writes=rXF, sem=sXF)

        for blk in range(8):
            wt, wr = next_wbuf()
            wsem = sWA if wt is WAB[0] else sWB
            wsrc = wv[:, blk * 256:(blk + 1) * 256].rearrange(
                "(c p) n -> p c n", p=128)
            dma([(wt[:].rearrange("p (c n) -> p c n", n=256), wsrc)],
                writes=[wr], sem=wsem)
            for tc in range(4):
                for kc in range(16):
                    mm(P[tc][:, 0:256], rP[tc],
                       XF[:, kc * SEG + tc * 128:kc * SEG + (tc + 1) * 128],
                       wt[:, kc * 256:(kc + 1) * 256], kc == 0, kc == 15,
                       reads=[rXF[kc], wr])
                sop(lambda e, tc=tc, blk=blk: e.copy(
                    VT[tc][:, blk * 256:(blk + 1) * 256], P[tc][:, 0:256]),
                    reads=[rP[tc]], writes=[rVT[tc]])

        def emit_prep(h):
            wt, wr = next_wbuf()
            qsrc = wq[:, h * 128:(h + 1) * 128].rearrange(
                "(c p) n -> p c n", p=128)
            ksrc = wk[:, h * 128:(h + 1) * 128].rearrange(
                "(c p) n -> p c n", p=128)
            wview = wt[:].rearrange("p (c n) -> p c n", n=256)
            wsem = sWA if wt is WAB[0] else sWB
            dma([(wview[:, :, 0:128], qsrc), (wview[:, :, 128:256], ksrc)],
                writes=[wr], sem=wsem)
            for kc in range(16):
                mm(P[5][:], rP[5], wt[:, kc * 256:kc * 256 + 128],
                   XF[:, kc * SEG:(kc + 1) * SEG], kc == 0, kc == 15,
                   reads=[rXF[kc], wr])
                mm(P[6][:], rP[6], wt[:, kc * 256 + 128:(kc + 1) * 256],
                   XF[:, kc * SEG:(kc + 1) * SEG], kc == 0, kc == 15,
                   reads=[rXF[kc], wr])
            p = h % 2
            sop(lambda e, p=p: e.copy(QHs[p][:], P[5][:]),
                reads=[rP[5]], writes=[rQH[p]])
            sop(lambda e, p=p: e.copy(KHs[p][:], P[6][:]),
                reads=[rP[6]], writes=[rKH[p]])

        emit_prep(0)
        for h in range(H):
            p = h % 2
            QH, KH, SQt, SKt = QHs[p], KHs[p], SQs[p], SKs[p]
            rQHp, rKHp, rSQp, rSKp = rQH[p], rKH[p], rSQ[p], rSK[p]
            if h + 1 < H:
                emit_prep(h + 1)
            emit_gather(2)

            def elu1(srcT, rsrc, dstT, rdst):
                vop(lambda e: e.tensor_scalar(SCR1[:], c32(srcT[:]), 0.0, None,
                                              OP.min),
                    reads=[rsrc], writes=[rSCR1])
                sop(lambda e: e.activation(SCR1[:], SCR1[:], AF.Exp,
                                           bias=ZCOL[:]),
                    reads=[rSCR1, rMISC], writes=[rSCR1])
                vop(lambda e: e.tensor_scalar(SCRB[:], c32(srcT[:]), 0.0, None,
                                              OP.max),
                    reads=[rsrc], writes=[rSCRB])
                vop(lambda e: e.tensor_tensor(dstT[:], SCR1[:], SCRB[:],
                                              OP.add),
                    reads=[rSCR1, rSCRB], writes=[rdst])

            elu1(QH, rQHp, SQt, rSQp)
            elu1(KH, rKHp, SKt, rSKp)
            emit_gather(2)

            vop(lambda e: e.reduce_max(MKC[:], c32(KH[:]), axis=AX.X,
                                       apply_absolute_value=True),
                reads=[rKHp], writes=[rMISC])
            vop(lambda e: e.scalar_tensor_tensor(
                SCR1[:], c32(QH[:]), -1.0, c32(QH[:]), OP.mult, OP.max),
                reads=[rQHp], writes=[rSCR1])
            vop(lambda e: e.tensor_scalar(SQR[:], SCR1[:], MKC[:], None,
                                          OP.mult),
                reads=[rSCR1, rMISC], writes=[rSQR])
            mm(P[2][0:2, :], rP[2], ONEC[:], SQR[:], True, True,
               reads=[rSQR, rMISC])
            vop(lambda e: e.tensor_scalar(ROWR[0:1, 0:SEG],
                                          P[2][0:1, :], -1.0, None, OP.mult),
                reads=[rP[2]], writes=[rROWR])
            emit_gather(2)

            mm(P[4][:], rP[4], MEMZ[:, h * 132:h * 132 + 128], SQt[:], True,
               True, reads=[rMEMZ[h], rSQp])
            mm(P[2][0:2, :], rP[2], MEMZ[:, h * 132 + 128:h * 132 + 130],
               SQt[:], True, True, reads=[rMEMZ[h], rSQp, rROWR])
            sop(lambda e: e.activation(ROWF[0:1, 0:SEG], P[2][0:1, :],
                                       AF.Ln, bias=ZCOL[0:1, :]),
                reads=[rP[2], rMISC], writes=[rROWF])
            sop(lambda e: e.activation(ROWR[0:1, SEG:2 * SEG],
                                       ROWF[0:1, 0:SEG], AF.Exp,
                                       bias=ZCOL[0:1, :], scale=-1.0),
                reads=[rROWF, rMISC], writes=[rROWR])
            mm(P[2][:], rP[2], ONER[:], ROWR[0:1, SEG:2 * SEG], True, True,
               reads=[rROWR, rMISC])
            sop(lambda e: e.copy(SCRB[:], P[2][:]), reads=[rP[2]],
                writes=[rSCRB])
            vop(lambda e: e.tensor_tensor(AM[:], P[4][:], SCRB[:], OP.mult),
                reads=[rP[4], rSCRB], writes=[rAM])
            vop(lambda e, h=h: e.tensor_scalar(AM[:], AM[:], BET[:, h:h + 1],
                                               None, OP.mult),
                reads=[rAM, rMISC], writes=[rAM])
            emit_gather(2)

            for tc in range(4):
                bank = tc % 2
                mm(P[bank][:], rP[bank], KH[:, tc * 128:(tc + 1) * 128],
                   QH[:], True, False, reads=[rKHp, rQHp])
                mm(P[bank][:], rP[bank], ONER[:], ROWR[0:1, 0:SEG], False,
                   True, reads=[rROWR, rMISC])
                sop(lambda e, b=bank: e.activation(SCR1[:], P[b][:], AF.Exp,
                                                   bias=ZCOL[:], scale=SCALE),
                    reads=[rP[bank], rMISC], writes=[rSCR1])
                eb = tc % 2
                vop(lambda e, tc=tc, eb=eb: e.tensor_tensor(
                    EXPb[:, eb * SEG:(eb + 1) * SEG], SCR1[:],
                    MSK[:, tc * SEG:(tc + 1) * SEG], OP.mult),
                    reads=[rSCR1, rMISC], writes=[rEXP[eb]])
                emit_gather(2)
                mm(P[4][:], rP[4], VT[tc][:, h * 128:(h + 1) * 128],
                   EXPb[:, eb * SEG:(eb + 1) * SEG], tc == 0, tc == 3,
                   reads=[rVT[tc], rEXP[eb]])
                mm(P[2][0:2, :], rP[2], ONEC[:],
                   EXPb[:, eb * SEG:(eb + 1) * SEG], tc == 0, tc == 3,
                   reads=[rEXP[eb], rMISC])

            sop(lambda e: e.activation(ROWF[0:1, SEG:2 * SEG], P[2][0:1, :],
                                       AF.Ln, bias=ZCOL[0:1, :]),
                reads=[rP[2], rMISC], writes=[rROWF])
            sop(lambda e: e.activation(ROWR[0:1, 0:SEG],
                                       ROWF[0:1, SEG:2 * SEG], AF.Exp,
                                       bias=ZCOL[0:1, :], scale=-1.0),
                reads=[rROWF, rMISC], writes=[rROWR])
            mm(P[2][:], rP[2], ONER[:], ROWR[0:1, 0:SEG], True, True,
               reads=[rROWR, rMISC])
            sop(lambda e: e.copy(SCR1[:], P[2][:]), reads=[rP[2]],
                writes=[rSCR1])
            vop(lambda e: e.tensor_tensor(SCRB[:], P[4][:], SCR1[:], OP.mult),
                reads=[rP[4], rSCR1], writes=[rSCRB])
            emit_gather(2)
            scat = ATT[:].rearrange("p (c x) -> p x c", c=16)[
                :, h * 32:h * 32 + 32, :]
            vop(lambda e, h=h, scat=scat: e.scalar_tensor_tensor(
                scat, SCRB[:], OMB[:, h:h + 1],
                AM[:], OP.mult, OP.add),
                reads=[rSCRB, rAM, rMISC], writes=rATT)
            emit_gather(2)

            for tc in range(4):
                mm(P[3][:, 0:130], rP[3], SKt[:, tc * 128:(tc + 1) * 128],
                   MEMZ[:, h * 132:h * 132 + 130], True, True,
                   reads=[rSKp, rMEMZ[h]])
                vop(lambda e: e.reciprocal(RCP[:], P[3][:, 128:129]),
                    reads=[rP[3]], writes=[rRCP])
                vop(lambda e: e.tensor_scalar(SCR1[:, 0:128], P[3][:, 0:128],
                                              RCP[:], None, OP.mult),
                    reads=[rP[3], rRCP], writes=[rSCR1])
                vop(lambda e, tc=tc, h=h: e.tensor_tensor(
                    DLT[:, tc * 128:(tc + 1) * 128],
                    c32(VT[tc][:, h * 128:(h + 1) * 128]), SCR1[:, 0:128],
                    OP.subtract),
                    reads=[rVT[tc], rSCR1], writes=[rDLT, rEXP[0]])
                kb.ei("tensor", lambda e, tc=tc, SKt=SKt: e.transpose(
                    P[3][:, 256:384].bitcast(F32R),
                    SKt[:, tc * 128:(tc + 1) * 128], IDENT[:]),
                    reads=[rSKp, rMISC], writes=[rP[3]], sem=sT)
                sop(lambda e, tc=tc: e.copy(SKT[:, tc * 128:(tc + 1) * 128],
                                            P[3][:, 256:384]),
                    reads=[rP[3]], writes=[rSKT, rEXP[1]])
                emit_gather(2)
                mm(P[7][:, 0:128], rP[7], SKT[:, tc * 128:(tc + 1) * 128],
                   DLT[:, tc * 128:(tc + 1) * 128], tc == 0, tc == 3,
                   reads=[rSKT, rDLT, rEXP[0], rEXP[1]])
            vop(lambda e, h=h: e.tensor_tensor(
                MEMZ[:, h * 132:h * 132 + 128],
                c32(MEMZ[:, h * 132:h * 132 + 128]), P[7][:, 0:128], OP.add),
                reads=[rP[7]], writes=[rMEMZ[h]])
            vop(lambda e, SKt=SKt: e.reduce_sum(SCR1[:, 0:1], c32(SKt[:]),
                                                axis=AX.X),
                reads=[rSKp], writes=[rSCR1])
            vop(lambda e, h=h: e.tensor_tensor(
                MEMZ[:, h * 132 + 128:h * 132 + 129],
                c32(MEMZ[:, h * 132 + 128:h * 132 + 129]), SCR1[:, 0:1],
                OP.add), reads=[rSCR1], writes=[rMEMZ[h]])
            emit_gather(2)

        for blk in range(8):
            wt, wr = next_wbuf()
            wsrc = wo[:, blk * 256:(blk + 1) * 256].rearrange(
                "(c p) n -> p c n", p=128)
            wsem = sWA if wt is WAB[0] else sWB
            dma([(wt[:].rearrange("p (c n) -> p c n", n=256), wsrc)],
                writes=[wr], sem=wsem)
            for dcl in range(2):
                dc = blk * 2 + dcl
                bank = dcl
                for hc in range(16):
                    mm(P[bank][:], rP[bank],
                       wt[:, hc * 256 + dcl * 128:hc * 256 + (dcl + 1) * 128],
                       ATT[:, hc * SEG:(hc + 1) * SEG], hc == 0, hc == 15,
                       reads=[rATT[hc], wr])
                vop(lambda e, dc=dc, b=bank: e.tensor_tensor(
                    XF[:, dc * SEG:(dc + 1) * SEG],
                    c32(XF[:, dc * SEG:(dc + 1) * SEG]), P[b][:], OP.add),
                    reads=[rP[bank]], writes=[rXF[dc]])
                sop(lambda e, dc=dc: e.activation(
                    SQR[:], c32(XF[:, dc * SEG:(dc + 1) * SEG]),
                    AF.Square, bias=ZCOL[:]),
                    reads=[rXF[dc], rMISC], writes=[rSQR])
                mm(P[2][0:2, :], rP[2], ONEC[:], XF[:, dc * SEG:(dc + 1) * SEG],
                   dc == 0, dc == 15, reads=[rXF[dc], rMISC])
                mm(P[3][0:2, :], rP[3], ONEC[:], SQR[:], dc == 0, dc == 15,
                   reads=[rSQR, rMISC], force_sem=True)
        ln_and_store(sg, store_y=True)

    GROUPS = [(1, 4), (5, 8), (9, 12), (13, 15)]

    class GTile:
        __slots__ = ("r0", "nr", "tf", "dst", "rdst", "srcT", "ncols", "gi",
                     "slots", "stoks", "vtoks", "coll", "ai")

    _gt_state = {"gi": 0, "slot": 0}
    _prev_tile = [None, None, None, None]

    def tile_list():
        tiles = []
        arrays = [(wg_sh, rWGH[0], gidxT, D // 8, INTER, 0),
                  (wu_sh, rWGH[1], uidxT, D // 8, INTER, 1),
                  (wd_sh, rWGH[2], didxT, INTER // 8, D, 2)]
        for dst_sh, rdst, src_idxT, nrows, ncols, ai in arrays:
            r0 = 0
            atiles = []
            while r0 < nrows:
                nr = min(128, nrows - r0)
                for tf in range((ncols + GT - 1) // GT):
                    t = GTile()
                    t.r0, t.nr, t.tf = r0, nr, tf
                    t.dst, t.rdst, t.srcT, t.ncols = dst_sh, rdst, src_idxT, ncols
                    t.stoks, t.vtoks = [], []
                    t.coll = None
                    t.ai = ai
                    atiles.append(t)
                r0 += nr
            atiles[-1].coll = ai
            tiles.extend(atiles)
        return tiles

    sCC = kb.sem("sCC")

    def _emit_collective(ai):
        shard, full, rres = [
            (wg_shT, wg_hbmT, rWGH[0]),
            (wu_shT, wu_hbmT, rWGH[1]),
            (wd_shT, wd_hbmT, rWGH[2])][ai]
        toks = [rres.w] + rres.rs + [(sCC, sCC.n)]
        kb.waits("gpsimd", toks)

        def fn(e, shard=shard, full=full):
            return e.collective_compute(
                "AllGather", OP.bypass,
                replica_groups=[list(range(8))],
                ins=[shard.ap().opt()], outs=[full.ap().opt()])
        sCC.n += 1
        tok = (sCC, sCC.n)
        kb.q("gpsimd", lambda e, fn=fn, v=sCC.n: fn(e).then_inc(sCC.h, 1))
        rres.w = tok
        rres.rs = []

    if dbg:
        wgdump = nc.dram_tensor("wgdump", [D, INTER], F32R,
                                kind="ExternalOutput").ap()
        wddump = nc.dram_tensor("wddump", [INTER, D], F32R,
                                kind="ExternalOutput").ap()

    _all_tiles = tile_list()
    _tiles = _all_tiles
    _gtiles = []
    _cur = {"t": 0, "g": 0}
    sG = kb.sem("sG")


    def emit_group(t, gidx_):
        f0 = t.tf * GT
        fw = min(GT, t.ncols - f0)
        g0, g1 = GROUPS[gidx_]
        if gidx_ == 0:
            t.gi = _gt_state["gi"] % 2
            _gt_state["gi"] += 1
            t.slots = []
            dma([(GIX[t.gi][0:t.nr, 0:fw],
                  t.srcT[t.r0:t.r0 + t.nr, f0:f0 + fw])],
                writes=[rGIX[t.gi]], sem=sGXA if t.gi == 0 else sGXB)
        prev = _prev_tile[gidx_]
        ew = []
        if prev is not None and len(prev.vtoks) > gidx_:
            ew = [prev.vtoks[gidx_]]
        base = len(t.slots)
        for j in range(g0, g1 + 1):
            gs = _gt_state["slot"] % 15
            _gt_state["slot"] += 1
            t.slots.append(gs)
            sop(lambda e, j=j, gs=gs, fw=fw, gi=t.gi, nr=t.nr:
                e.activation(GS[gs][0:nr, 0:fw], GIX[gi][0:nr, 0:fw],
                             AF.Sign, bias=SGNB[0:nr, j - 1:j]),
                reads=[rGIX[t.gi], rMISC], writes=[rGS[gs]],
                sem=sS,
                extra_waits=ew if j == g0 else ())
        t.stoks.append((sS, sS.n))
        _prev_tile[gidx_] = t
        gact = OUTt if t.gi == 0 else GACB
        rgac = rGAC[t.gi]
        si = base
        for j in range(g0, g1 + 1):
            gs = t.slots[si]
            si += 1
            if j == 1:
                vop(lambda e, gs=gs, fw=fw, ga=gact, nr=t.nr:
                    e.tensor_scalar(ga[0:nr, 0:fw], GS[gs][0:nr, 0:fw],
                                    DIF[0:nr, 1:2], CC[0:nr, :],
                                    OP.mult, OP.add),
                    reads=[rGS[gs], rMISC], writes=[rgac],
                    sem=sV,
                    extra_waits=[t.stoks[-1]] if j == g0 else ())
            else:
                vop(lambda e, j=j, gs=gs, fw=fw, ga=gact, nr=t.nr:
                    e.scalar_tensor_tensor(
                        ga[0:nr, 0:fw], GS[gs][0:nr, 0:fw],
                        DIF[0:nr, j:j + 1],
                        ga[0:nr, 0:fw], OP.mult, OP.add),
                    reads=[rGS[gs], rMISC, rgac], writes=[rgac],
                    sem=sV,
                    extra_waits=[t.stoks[-1]] if j == g0 else ())
        t.vtoks.append((sV, sV.n))
        if gidx_ == len(GROUPS) - 1:
            dma([(t.dst[t.r0:t.r0 + t.nr, f0:f0 + fw],
                  gact[0:t.nr, 0:fw].bitcast(F32R))],
                reads=[rgac], writes=[t.rdst], sem=sWST[t.ai])
            if t.coll is not None:
                _emit_collective(t.coll)

    def emit_gather(n):
        for _ in range(n):
            ti = _cur["t"]
            if ti >= len(_tiles):
                return
            emit_group(_tiles[ti], _cur["g"])
            _cur["g"] += 1
            if _cur["g"] == len(GROUPS):
                _cur["g"] = 0
                _cur["t"] += 1

    def drain_gather():
        emit_gather(len(_tiles) * len(GROUPS))

    for sg in range(NSEG):
        seg_attention(sg)

    drain_gather()
    if dbg:
        kb.waits("gpsimd", [(sCC, sCC.n)])
        kb.q("gpsimd", lambda e: e.dma_start(out=wgdump, in_=wg_hbm)
             .then_inc(sDM.h, 16))
        kb.q("gpsimd", lambda e: e.dma_start(out=wddump, in_=wd_hbm)
             .then_inc(sDM.h, 16))
        sDM.n += 32

    HT = [(SQs[0], rSQ[0]), (SKs[0], rSK[0])]

    def ffn_chunk(ch):
        t0 = ch * SEG
        ysrc = y_hbm[:, t0:t0 + SEG].rearrange("(c p) s -> p c s", p=128)
        dma([(XF[:].rearrange("p (c s) -> p c s", s=SEG), ysrc)],
            reads=[rY], writes=rXF, sem=sXF)

        for oc in range(NOC):
            i = oc % 2
            wt, wr = WAB[i], rWAB[i]
            gsrc = wg_hbm[:, oc * 128:(oc + 1) * 128].rearrange(
                "(c p) n -> p c n", p=128)
            usrc = wu_hbm[:, oc * 128:(oc + 1) * 128].rearrange(
                "(c p) n -> p c n", p=128)
            wview = wt[:].rearrange("p (c n) -> p c n", n=256)
            wsem = sWA if wt is WAB[0] else sWB
            vtb = VT[oc % 4]
            rvtb = rVT[oc % 4]
            dma([(wview[:, :, 0:128], gsrc), (wview[:, :, 128:256], usrc)],
                reads=[rWGH[0], rWGH[1]], writes=[wr], sem=wsem)
            dma([(vtb[:, :], wd_hbm[oc * 128:(oc + 1) * 128, :])],
                reads=[rWGH[2]], writes=[rvtb], sem=sVT[oc % 4])

            pg, pu = P[0 + i], P[2 + i]
            for kc in range(16):
                mm(pg[:], rP[0 + i], wt[:, kc * 256:kc * 256 + 128],
                   XF[:, kc * SEG:(kc + 1) * SEG], kc == 0, kc == 15,
                   reads=[rXF[kc], wr])
                mm(pu[:], rP[2 + i], wt[:, kc * 256 + 128:(kc + 1) * 256],
                   XF[:, kc * SEG:(kc + 1) * SEG], kc == 0, kc == 15,
                   reads=[rXF[kc], wr])
            sop(lambda e, i=i: e.activation(SCR1[:], P[0 + i][:], AF.Silu,
                                            bias=ZCOL[:]),
                reads=[rP[0 + i], rMISC], writes=[rSCR1])
            ht, rht = HT[i]
            vop(lambda e, i=i, ht=ht: e.tensor_tensor(
                ht[:], SCR1[:], P[2 + i][:], OP.mult),
                reads=[rSCR1, rP[2 + i]], writes=[rht])

            if oc % 2 == 1 or oc == NOC - 1:
                ocs = [oc] if oc % 2 == 0 else [oc - 1, oc]
                for dc in range(16):
                    bank = 4 + dc % 2
                    for ii, oci in enumerate(ocs):
                        hti, rhti = HT[oci % 2]
                        mm(P[bank][:], rP[bank],
                           VT[oci % 4][:, dc * 128:(dc + 1) * 128], hti[:],
                           ii == 0, ii == len(ocs) - 1,
                           reads=[rVT[oci % 4], rhti])
                    if oc <= 1:
                        vop(lambda e, dc=dc, b=bank: e.tensor_copy(
                            ATT[:, dc * SEG:(dc + 1) * SEG], P[b][:]),
                            reads=[rP[bank]], writes=[rATT[dc]])
                    else:
                        vop(lambda e, dc=dc, b=bank: e.tensor_tensor(
                            ATT[:, dc * SEG:(dc + 1) * SEG],
                            c32(ATT[:, dc * SEG:(dc + 1) * SEG]), P[b][:],
                            OP.add), reads=[rP[bank]], writes=[rATT[dc]])

        for dc in range(16):
            vop(lambda e, dc=dc: e.tensor_tensor(
                XF[:, dc * SEG:(dc + 1) * SEG],
                c32(XF[:, dc * SEG:(dc + 1) * SEG]),
                c32(ATT[:, dc * SEG:(dc + 1) * SEG]), OP.add),
                reads=[rATT[dc]], writes=[rXF[dc]])
            sop(lambda e, dc=dc: e.activation(
                SQR[:], c32(XF[:, dc * SEG:(dc + 1) * SEG]), AF.Square,
                bias=ZCOL[:]), reads=[rXF[dc], rMISC], writes=[rSQR])
            mm(P[2][0:2, :], rP[2], ONEC[:], XF[:, dc * SEG:(dc + 1) * SEG],
               dc == 0, dc == 15, reads=[rXF[dc], rMISC])
            mm(P[3][0:2, :], rP[3], ONEC[:], SQR[:], dc == 0, dc == 15,
               reads=[rSQR, rMISC], force_sem=True)
        ln_and_store(ch, store_y=False)

    for ch in range(NSEG):
        ffn_chunk(ch)

    for _sm in [sDM, sXF, sWA, sWB, sGXA, sGXB, sYS, sOU] + sVT + sWST:
        if _sm.n:
            kb.q("sync", lambda e, s=_sm: e.wait_ge(s.h, s.n))
    kb.q("sync", lambda e: e.wait_ge(sCC.h, sCC.n))

    with ctx:
        with ExitStack() as semctx:
            for s in kb.sems.values():
                s.h = semctx.enter_context(nc.semaphore(s.name))
            with nc.Block() as block:
                kb.flush(block)
    return nc


def _host_prep(inputs):
    x = np.asarray(inputs["L0_attn_x"], dtype=np.float32)
    wq = np.ascontiguousarray(np.asarray(inputs["L0_attn_Wq"], np.float32))
    wk = np.ascontiguousarray(np.asarray(inputs["L0_attn_Wk"], np.float32))
    wv = np.ascontiguousarray(np.asarray(inputs["L0_attn_Wv"], np.float32))
    wo = np.ascontiguousarray(np.asarray(inputs["L0_attn_Wo"], np.float32))
    betas = np.asarray(inputs["L0_attn_betas"], np.float32)
    lut = np.asarray(inputs["L0_ffn_lut"], np.float32)
    gidx = np.asarray(inputs["L0_ffn_gate_idx"], np.int32)
    uidx = np.asarray(inputs["L0_ffn_up_idx"], np.int32)
    didx = np.asarray(inputs["L0_ffn_down_idx"], np.int32)

    memz0 = np.zeros((128, H * 132), dtype=np.float32)
    for h in range(H):
        memz0[:, h * 132 + 128] = 1.0 / 128.0

    maskT = np.zeros((128, 4 * SEG), dtype=np.float32)
    for tc in range(4):
        tl = np.arange(128)[:, None]
        s = np.arange(SEG)[None, :]
        maskT[:, tc * SEG:(tc + 1) * SEG] = (s >= (tc * 128 + tl)).astype(
            np.float32)

    shared = {
        "wq": wq, "wk": wk, "wv": wv, "wo": wo,

        "lutb": np.ascontiguousarray(np.broadcast_to(lut, (128, 16))),
        "betaTin": np.ascontiguousarray(betas[0, :, 0, :].T),
        "identin": np.eye(128, dtype=np.float32),
        "maskin": maskT.astype(ml_dtypes.bfloat16),
        "onecolin": np.ones((128, 2), np.float32),
        "onerowin": np.ones((1, 128), np.float32),
        "sgnin": np.ascontiguousarray(np.broadcast_to(
            (0.5 - np.arange(1, 16, dtype=np.float32))[None, :], (128, 15))),
        "memzin": memz0,
    }
    gidxT = np.ascontiguousarray(gidx.T)
    uidxT = np.ascontiguousarray(uidx.T)
    didxT = np.ascontiguousarray(didx.T)
    in_maps = []
    for b in range(8):
        m = dict(shared)
        m["xT"] = np.ascontiguousarray(x[b].T)
        m["gidxT"] = np.ascontiguousarray(gidxT[b * 256:(b + 1) * 256])
        m["uidxT"] = np.ascontiguousarray(uidxT[b * 256:(b + 1) * 256])
        m["didxT"] = np.ascontiguousarray(didxT[b * 688:(b + 1) * 688])
        in_maps.append(m)
    return in_maps


def kernel(**inputs):
    in_maps = _host_prep(inputs)
    nc = build_nc()
    res = run_bass_kernel_spmd(nc, in_maps, core_ids=list(range(8)))
    out = np.stack([np.ascontiguousarray(np.asarray(r["outT"]).T)
                    for r in res.results])
    return out.astype(np.float32)


if __name__ == "__main__":
    nc = build_nc()
    print("built ok")


# version 23
# speedup vs baseline: 1.0060x; 1.0060x over previous
import numpy as np
import ml_dtypes
from contextlib import ExitStack

import concourse.bass as bass
import concourse.mybir as mybir
from concourse.bass_utils import run_bass_kernel_spmd

F32 = mybir.dt.float32
F32R = mybir.dt.float32r
I32 = mybir.dt.int32
OP = mybir.AluOpType
AF = mybir.ActivationFunctionType
AX = mybir.AxisListType

D = 2048
S = 2048
SEG = 512
NSEG = 4
H = 16
INTER = 5504
NOC = INTER // 128
SCALE = 1.0 / float(np.sqrt(128.0))
EPS = 1e-5
GT = 512

ENGS = ("sync", "tensor", "vector", "scalar", "gpsimd")


class Sem:
    __slots__ = ("name", "h", "n")

    def __init__(self, name):
        self.name = name
        self.h = None
        self.n = 0


class Res:
    __slots__ = ("w", "rs")

    def __init__(self):
        self.w = None
        self.rs = []


class KB:
    def __init__(self, nc):
        self.nc = nc
        self.steps = []
        self.sems = {}
        self.waited = {e: {} for e in ENGS}
        self.pend = {e: [] for e in ENGS}

    def sem(self, name):
        if name not in self.sems:
            self.sems[name] = Sem(name)
        return self.sems[name]

    def q(self, eng, fn):
        self.steps.append((eng, fn))

    def wait(self, eng, tok):
        if tok is None:
            return
        sem, val = tok
        if val <= 0:
            return
        wd = self.waited[eng]
        if wd.get(sem.name, 0) >= val:
            return
        wd[sem.name] = val
        self.q(eng, lambda e, s=sem, v=val: e.wait_ge(s.h, v))

    def waits(self, eng, toks):
        best = {}
        for t in toks:
            if t is None:
                continue
            s, v = t
            if s.name not in best or best[s.name][1] < v:
                best[s.name] = t
        for t in best.values():
            self.wait(eng, t)

    def ei(self, eng, fn, reads=(), writes=(), sem=None, inc_n=1,
           extra_waits=()):
        toks = []
        for r in reads:
            toks.append(r.w)
        for r in writes:
            toks.append(r.w)
            toks.extend(r.rs)
        toks.extend(extra_waits)
        self.waits(eng, toks)
        tok = None
        if sem is not None:
            sem.n += inc_n
            tok = (sem, sem.n)

            def thunk(e, fn=fn, s=sem, n=inc_n):
                fn(e).then_inc(s.h, n)
            self.q(eng, thunk)
        else:
            self.q(eng, lambda e, fn=fn: fn(e))
        for r in writes:
            r.w = tok
            r.rs = []
        if tok is not None:
            for r in reads:
                r.rs.append(tok)
            for rds, wts in self.pend[eng]:
                for r in rds:
                    r.rs.append(tok)
                for r in wts:
                    if r.w is None:
                        r.w = tok
            self.pend[eng] = []
        else:
            self.pend[eng].append((list(reads), list(writes)))
        return tok

    def flush(self, block):
        for eng in ENGS:
            fns = [f for e, f in self.steps if e == eng]
            if not fns:
                continue

            def section(e, fns=fns):
                for f in fns:
                    f(e)
            getattr(block, eng)(section)


def build_nc(dbg=False):
    nc = bass.Bass("TRN2", target_bir_lowering=False)

    xT = nc.dram_tensor("xT", [D, S], F32R, kind="ExternalInput").ap()
    wq = nc.dram_tensor("wq", [D, D], F32R, kind="ExternalInput").ap()
    wk = nc.dram_tensor("wk", [D, D], F32R, kind="ExternalInput").ap()
    wv = nc.dram_tensor("wv", [D, D], F32R, kind="ExternalInput").ap()
    wo = nc.dram_tensor("wo", [D, D], F32R, kind="ExternalInput").ap()
    gidxT = nc.dram_tensor("gidxT", [D // 8, INTER], I32,
                           kind="ExternalInput").ap()
    uidxT = nc.dram_tensor("uidxT", [D // 8, INTER], I32,
                           kind="ExternalInput").ap()
    didxT = nc.dram_tensor("didxT", [INTER // 8, D], I32,
                           kind="ExternalInput").ap()
    lutb = nc.dram_tensor("lutb", [128, 16], F32, kind="ExternalInput").ap()
    betaTin = nc.dram_tensor("betaTin", [128, H], F32, kind="ExternalInput").ap()
    identin = nc.dram_tensor("identin", [128, 128], F32R,
                             kind="ExternalInput").ap()
    maskin = nc.dram_tensor("maskin", [128, 4 * SEG], mybir.dt.bfloat16,
                            kind="ExternalInput").ap()
    onecolin = nc.dram_tensor("onecolin", [128, 2], F32R,
                              kind="ExternalInput").ap()
    onerowin = nc.dram_tensor("onerowin", [1, 128], F32R,
                              kind="ExternalInput").ap()
    sgnin = nc.dram_tensor("sgnin", [128, 15], F32, kind="ExternalInput").ap()
    memzin = nc.dram_tensor("memzin", [128, H * 132], F32R,
                            kind="ExternalInput").ap()

    outT = nc.dram_tensor("outT", [D, S], F32, kind="ExternalOutput").ap()

    ikind = "ExternalOutput"
    y_hbm = nc.dram_tensor("y_hbm", [D, S], F32R, kind=ikind).ap()
    wg_hbmT = nc.dram_tensor("wg_hbm", [D, INTER], F32R, kind="Internal")
    wu_hbmT = nc.dram_tensor("wu_hbm", [D, INTER], F32R, kind="Internal")
    wd_hbmT = nc.dram_tensor("wd_hbm", [INTER, D], F32R, kind="Internal")
    wg_shT = nc.dram_tensor("wg_sh", [D // 8, INTER], F32R, kind="Internal")
    wu_shT = nc.dram_tensor("wu_sh", [D // 8, INTER], F32R, kind="Internal")
    wd_shT = nc.dram_tensor("wd_sh", [INTER // 8, D], F32R, kind="Internal")
    wg_hbm, wu_hbm, wd_hbm = wg_hbmT.ap(), wu_hbmT.ap(), wd_hbmT.ap()
    wg_sh, wu_sh, wd_sh = wg_shT.ap(), wu_shT.ap(), wd_shT.ap()

    ctx = ExitStack()

    def sb(name, shape, dt):
        return ctx.enter_context(nc.sbuf_tensor(name, shape, dt))

    def psum(name, shape):
        return ctx.enter_context(nc.psum_tensor(name, shape, F32))

    XF = sb("XF", [128, 16 * SEG], F32R)
    ATT = sb("ATT", [128, 16 * SEG], F32R)
    VT = [sb(f"VT{i}", [128, 2048], F32R) for i in range(4)]
    WAB = [sb("WA", [128, 4096], F32R), sb("WB", [128, 4096], F32R)]
    QHs = [sb("QH0", [128, SEG], F32R), sb("QH1", [128, SEG], F32R)]
    KHs = [sb("KH0", [128, SEG], F32R), sb("KH1", [128, SEG], F32R)]
    SQs = [sb("SQ0", [128, SEG], F32R), sb("SQ1", [128, SEG], F32R)]
    SKs = [sb("SK0", [128, SEG], F32R), sb("SK1", [128, SEG], F32R)]
    EXPb = sb("EXPb", [128, 2 * SEG], F32R)
    DLT = EXPb[:, 0:SEG]
    SKT = EXPb[:, SEG:2 * SEG]
    GIXG = sb("GIXG", [128, GT], F32)
    GMSK = sb("GMSK", [128, GT], F32)
    GACG = sb("GACG", [128, GT], F32)
    MEMZ = sb("MEMZ", [128, H * 132], F32R)
    SQR = sb("SQR", [128, SEG], F32R)
    SCR1 = sb("SCR1", [128, SEG], F32)
    SCRB = sb("SCRB", [128, SEG], F32)
    AM = sb("AM", [128, SEG], F32)
    OUTt = sb("OUTt", [128, SEG], F32)
    GIX = [sb("GIXA", [128, GT], I32), sb("GIXB", [128, GT], I32)]
    GS = [sb(f"GS{i}", [128, GT], mybir.dt.bfloat16) for i in range(15)]
    GACB = sb("GACB", [128, GT], F32)
    MSK = sb("MSK", [128, 4 * SEG], mybir.dt.bfloat16)
    IDENT = sb("IDENT", [128, 128], F32R)
    LUT = sb("LUT", [128, 16], F32)
    DIF = sb("DIF", [128, 16], F32)
    DIF2 = sb("DIF2", [128, 16], F32)
    CC = sb("CC", [128, 1], F32)
    BET = sb("BET", [128, H], F32)
    OMB = sb("OMB", [128, H], F32)
    BTI = sb("BTI", [128, H], F32)
    MKC = sb("MKC", [128, 1], F32)
    RCP = sb("RCP", [128, 1], F32)
    ZCOL = sb("ZCOL", [128, 1], F32)
    SGNB = sb("SGNB", [128, 15], F32)
    ONEC = sb("ONEC", [128, 2], F32R)
    ONER = sb("ONER", [1, 128], F32R)
    ROWF = sb("ROWF", [1, 2 * SEG], F32)
    ROWR = sb("ROWR", [1, 2 * SEG], F32R)

    P = [psum(f"P{i}", [128, 512]) for i in range(8)]

    def c32(ap):
        return ap.bitcast(F32)

    kb = KB(nc)
    sDM = kb.sem("sDM")
    sXF = kb.sem("sXF")
    sWA = kb.sem("sWA")
    sWB = kb.sem("sWB")
    sVT = [kb.sem(f"sVT{i}") for i in range(4)]
    sGXA = kb.sem("sGXA")
    sGXB = kb.sem("sGXB")
    sWST = [kb.sem(f"sWST{i}") for i in range(3)]
    sYS = kb.sem("sYS")
    sOU = kb.sem("sOU")
    sT = kb.sem("sT")
    sV = kb.sem("sV")
    sS = kb.sem("sS")

    rXF = [Res() for _ in range(16)]
    rATT = [Res() for _ in range(16)]
    rVT = [Res() for _ in range(4)]
    rWAB = [Res(), Res()]
    rQH = [Res(), Res()]
    rKH = [Res(), Res()]
    rSQ = [Res(), Res()]
    rSK = [Res(), Res()]
    rSKT, rDLT = Res(), Res()
    rEXP = [Res(), Res()]
    rMEMZ = [Res() for _ in range(H)]
    rAQ, rSQR, rSCR1, rSCRB, rAM, rOUT = (Res() for _ in range(6))
    rGIX = [Res(), Res()]
    rGIXG, rGMSK, rGACG = Res(), Res(), Res()
    rGS = [Res() for _ in range(15)]
    rGAC = [rOUT, Res()]
    rP = [Res() for _ in range(8)]
    rROWF, rROWR = Res(), Res()
    rMISC = Res()
    rRCP = Res()
    rY = Res()
    rWGH = [Res() for _ in range(3)]

    def dma(pairs, reads=(), writes=(), sem=None):
        sm = sem if sem is not None else sDM

        def fn(e, sm=sm):
            inst = None
            for dst, src in pairs:
                inst = e.dma_start(out=dst, in_=src)
                inst.then_inc(sm.h, 16)
            return inst

        toks = []
        for r in reads:
            toks.append(r.w)
        for r in writes:
            toks.append(r.w)
            toks.extend(r.rs)
        kb.waits("sync", toks)
        sm.n += 16 * len(pairs)
        tok = (sm, sm.n)
        kb.q("sync", fn)
        for r in writes:
            r.w = tok
            r.rs = []
        for r in reads:
            r.rs.append(tok)
        return tok

    sDG = kb.sem("sDG")

    def gdma(pairs, reads=(), writes=()):
        def fn(e):
            inst = None
            for dst, srcp in pairs:
                inst = e.dma_start(out=dst, in_=srcp)
                inst.then_inc(sDG.h, 16)
            return inst
        toks = []
        for r in reads:
            toks.append(r.w)
        for r in writes:
            toks.append(r.w)
            toks.extend(r.rs)
        kb.waits("gpsimd", toks)
        sDG.n += 16 * len(pairs)
        tok = (sDG, sDG.n)
        kb.q("gpsimd", fn)
        for r in writes:
            r.w = tok
            r.rs = []
        for r in reads:
            r.rs.append(tok)
        return tok

    def mm(bank_ap, rres, lhsT, rhs, start, stop, reads, force_sem=False):
        def fn(e, o=bank_ap, l=lhsT, r=rhs, st=start, sp=stop):
            return e.matmul(o, l, r, start=st, stop=sp)
        return kb.ei("tensor", fn, reads=list(reads), writes=[rres],
                     sem=sT if (stop or force_sem) else None)

    def vop(fn, reads=(), writes=(), sem=sV, extra_waits=()):
        return kb.ei("vector", fn, reads=list(reads), writes=list(writes),
                     sem=sem, extra_waits=extra_waits)

    def sop(fn, reads=(), writes=(), sem=sS, extra_waits=()):
        return kb.ei("scalar", fn, reads=list(reads), writes=list(writes),
                     sem=sem, extra_waits=extra_waits)

    vop(lambda e: e.memset(ZCOL[:], 0.0), writes=[rMISC])
    dma([(MEMZ[:], memzin), (MSK[:], maskin), (IDENT[:], identin),
         (LUT[:], lutb), (BTI[:], betaTin), (ONEC[:], onecolin),
         (ONER[:], onerowin), (SGNB[:], sgnin)],
        writes=list(rMEMZ) + [rMISC])

    sop(lambda e: e.activation(BET[:], BTI[:], AF.Sigmoid, bias=ZCOL[:]),
        reads=[rMISC], writes=[rMISC])
    vop(lambda e: e.tensor_scalar(OMB[:], BET[:], -1.0, 1.0, OP.mult, OP.add),
        reads=[rMISC], writes=[rMISC])
    vop(lambda e: e.tensor_tensor(DIF[:, 1:16], LUT[:, 1:16], LUT[:, 0:15],
                                  OP.subtract), reads=[rMISC], writes=[rMISC])
    vop(lambda e: e.tensor_scalar(DIF[:, 1:16], DIF[:, 1:16], 0.5, None,
                                  OP.mult), reads=[rMISC], writes=[rMISC])
    vop(lambda e: e.tensor_tensor(DIF2[:, 1:16], DIF[:, 1:16], DIF[:, 1:16],
                                  OP.add), reads=[rMISC], writes=[rMISC])
    vop(lambda e: e.reduce_sum(CC[:], DIF[:, 1:16], axis=AX.X),
        reads=[rMISC], writes=[rMISC])
    vop(lambda e: e.tensor_tensor(CC[:], CC[:], LUT[:, 0:1], OP.add),
        reads=[rMISC], writes=[rMISC])

    wab_i = [0]

    def next_wbuf():
        i = wab_i[0] % 2
        wab_i[0] += 1
        return WAB[i], rWAB[i]

    def ln_and_store(sg, store_y):
        vop(lambda e: e.tensor_scalar(ROWF[0:1, 0:SEG], P[2][0:1, :],
                                      1.0 / 2048.0, None, OP.mult),
            reads=[rP[2]], writes=[rROWF])
        vop(lambda e: e.tensor_scalar(ROWF[0:1, SEG:2 * SEG], P[3][0:1, :],
                                      1.0 / 2048.0, None, OP.mult),
            reads=[rP[3]], writes=[rROWF])
        vop(lambda e: e.tensor_tensor(SCR1[0:1, 0:SEG],
                                      ROWF[0:1, 0:SEG], ROWF[0:1, 0:SEG],
                                      OP.mult), reads=[rROWF], writes=[rSCR1])
        vop(lambda e: e.tensor_tensor(ROWF[0:1, SEG:2 * SEG],
                                      ROWF[0:1, SEG:2 * SEG],
                                      SCR1[0:1, 0:SEG], OP.subtract),
            reads=[rROWF, rSCR1], writes=[rROWF])
        vop(lambda e: e.tensor_scalar(ROWF[0:1, SEG:2 * SEG],
                                      ROWF[0:1, SEG:2 * SEG], EPS, None,
                                      OP.add), reads=[rROWF], writes=[rROWF])
        sop(lambda e: e.activation(ROWF[0:1, SEG:2 * SEG],
                                   ROWF[0:1, SEG:2 * SEG], AF.Ln,
                                   bias=ZCOL[0:1, :]),
            reads=[rROWF, rMISC], writes=[rROWF])
        sop(lambda e: e.activation(ROWR[0:1, SEG:2 * SEG],
                                   ROWF[0:1, SEG:2 * SEG], AF.Exp,
                                   bias=ZCOL[0:1, :], scale=-0.5),
            reads=[rROWF, rMISC], writes=[rROWR])
        vop(lambda e: e.tensor_copy(ROWR[0:1, 0:SEG], ROWF[0:1, 0:SEG]),
            reads=[rROWF], writes=[rROWR])
        mm(P[4][:], rP[4], ONER[:], ROWR[0:1, 0:SEG], True, True,
           reads=[rROWR, rMISC])
        mm(P[5][:], rP[5], ONER[:], ROWR[0:1, SEG:2 * SEG], True, True,
           reads=[rROWR, rMISC])
        sop(lambda e: e.copy(AM[:], P[4][:]), reads=[rP[4]], writes=[rAM])
        sop(lambda e: e.copy(SCRB[:], P[5][:]), reads=[rP[5]], writes=[rSCRB])
        vop(lambda e: e.tensor_tensor(AM[:], AM[:], SCRB[:], OP.mult),
            reads=[rSCRB, rAM], writes=[rAM])
        for dc in range(16):
            vop(lambda e, dc=dc: e.tensor_tensor(
                SCR1[:], c32(XF[:, dc * SEG:(dc + 1) * SEG]), SCRB[:],
                OP.mult),
                reads=[rXF[dc], rSCRB], writes=[rSCR1])
            if store_y:
                vop(lambda e, dc=dc: e.tensor_tensor(
                    XF[:, dc * SEG:(dc + 1) * SEG], SCR1[:], AM[:],
                    OP.subtract), reads=[rSCR1, rAM], writes=[rXF[dc]])
            else:
                vop(lambda e: e.tensor_tensor(OUTt[:], SCR1[:], AM[:],
                                              OP.subtract),
                    reads=[rSCR1, rAM], writes=[rOUT])
                dma([(outT[dc * 128:(dc + 1) * 128, sg * SEG:(sg + 1) * SEG],
                      OUTt[:])], reads=[rOUT], sem=sOU)
        if store_y:
            dst = y_hbm[:, sg * SEG:(sg + 1) * SEG].rearrange(
                "(c p) s -> p c s", p=128)
            dma([(dst, XF[:].rearrange("p (c s) -> p c s", s=SEG))],
                reads=rXF, writes=[rY], sem=sYS)

    def seg_attention(sg):
        t0 = sg * SEG
        xsrc = xT[:, t0:t0 + SEG].rearrange("(c p) s -> p c s", p=128)
        xdst = XF[:].rearrange("p (c s) -> p c s", s=SEG)
        dma([(xdst, xsrc)], writes=rXF, sem=sXF)

        for blk in range(8):
            wt, wr = next_wbuf()
            wsem = sWA if wt is WAB[0] else sWB
            wsrc = wv[:, blk * 256:(blk + 1) * 256].rearrange(
                "(c p) n -> p c n", p=128)
            dma([(wt[:].rearrange("p (c n) -> p c n", n=256), wsrc)],
                writes=[wr], sem=wsem)
            for tc in range(4):
                for kc in range(16):
                    mm(P[tc][:, 0:256], rP[tc],
                       XF[:, kc * SEG + tc * 128:kc * SEG + (tc + 1) * 128],
                       wt[:, kc * 256:(kc + 1) * 256], kc == 0, kc == 15,
                       reads=[rXF[kc], wr])
                sop(lambda e, tc=tc, blk=blk: e.copy(
                    VT[tc][:, blk * 256:(blk + 1) * 256], P[tc][:, 0:256]),
                    reads=[rP[tc]], writes=[rVT[tc]])

        def emit_prep(h):
            wt, wr = next_wbuf()
            qsrc = wq[:, h * 128:(h + 1) * 128].rearrange(
                "(c p) n -> p c n", p=128)
            ksrc = wk[:, h * 128:(h + 1) * 128].rearrange(
                "(c p) n -> p c n", p=128)
            wview = wt[:].rearrange("p (c n) -> p c n", n=256)
            wsem = sWA if wt is WAB[0] else sWB
            dma([(wview[:, :, 0:128], qsrc), (wview[:, :, 128:256], ksrc)],
                writes=[wr], sem=wsem)
            for kc in range(16):
                mm(P[5][:], rP[5], wt[:, kc * 256:kc * 256 + 128],
                   XF[:, kc * SEG:(kc + 1) * SEG], kc == 0, kc == 15,
                   reads=[rXF[kc], wr])
                mm(P[6][:], rP[6], wt[:, kc * 256 + 128:(kc + 1) * 256],
                   XF[:, kc * SEG:(kc + 1) * SEG], kc == 0, kc == 15,
                   reads=[rXF[kc], wr])
            p = h % 2
            sop(lambda e, p=p: e.copy(QHs[p][:], P[5][:]),
                reads=[rP[5]], writes=[rQH[p]])
            sop(lambda e, p=p: e.copy(KHs[p][:], P[6][:]),
                reads=[rP[6]], writes=[rKH[p]])

        emit_prep(0)
        for h in range(H):
            p = h % 2
            QH, KH, SQt, SKt = QHs[p], KHs[p], SQs[p], SKs[p]
            rQHp, rKHp, rSQp, rSKp = rQH[p], rKH[p], rSQ[p], rSK[p]
            if h + 1 < H:
                emit_prep(h + 1)
            emit_gather(2)

            def elu1(srcT, rsrc, dstT, rdst):
                vop(lambda e: e.tensor_scalar(SCR1[:], c32(srcT[:]), 0.0, None,
                                              OP.min),
                    reads=[rsrc], writes=[rSCR1])
                sop(lambda e: e.activation(SCR1[:], SCR1[:], AF.Exp,
                                           bias=ZCOL[:]),
                    reads=[rSCR1, rMISC], writes=[rSCR1])
                vop(lambda e: e.tensor_scalar(SCRB[:], c32(srcT[:]), 0.0, None,
                                              OP.max),
                    reads=[rsrc], writes=[rSCRB])
                vop(lambda e: e.tensor_tensor(dstT[:], SCR1[:], SCRB[:],
                                              OP.add),
                    reads=[rSCR1, rSCRB], writes=[rdst])

            elu1(QH, rQHp, SQt, rSQp)
            elu1(KH, rKHp, SKt, rSKp)
            emit_gather(2)

            vop(lambda e: e.reduce_max(MKC[:], c32(KH[:]), axis=AX.X,
                                       apply_absolute_value=True),
                reads=[rKHp], writes=[rMISC])
            vop(lambda e: e.scalar_tensor_tensor(
                SCR1[:], c32(QH[:]), -1.0, c32(QH[:]), OP.mult, OP.max),
                reads=[rQHp], writes=[rSCR1])
            vop(lambda e: e.tensor_scalar(SQR[:], SCR1[:], MKC[:], None,
                                          OP.mult),
                reads=[rSCR1, rMISC], writes=[rSQR])
            mm(P[2][0:2, :], rP[2], ONEC[:], SQR[:], True, True,
               reads=[rSQR, rMISC])
            vop(lambda e: e.tensor_scalar(ROWR[0:1, 0:SEG],
                                          P[2][0:1, :], -1.0, None, OP.mult),
                reads=[rP[2]], writes=[rROWR])
            emit_gather(2)

            mm(P[4][:], rP[4], MEMZ[:, h * 132:h * 132 + 128], SQt[:], True,
               True, reads=[rMEMZ[h], rSQp])
            mm(P[2][0:2, :], rP[2], MEMZ[:, h * 132 + 128:h * 132 + 130],
               SQt[:], True, True, reads=[rMEMZ[h], rSQp, rROWR])
            sop(lambda e: e.activation(ROWF[0:1, 0:SEG], P[2][0:1, :],
                                       AF.Ln, bias=ZCOL[0:1, :]),
                reads=[rP[2], rMISC], writes=[rROWF])
            sop(lambda e: e.activation(ROWR[0:1, SEG:2 * SEG],
                                       ROWF[0:1, 0:SEG], AF.Exp,
                                       bias=ZCOL[0:1, :], scale=-1.0),
                reads=[rROWF, rMISC], writes=[rROWR])
            mm(P[2][:], rP[2], ONER[:], ROWR[0:1, SEG:2 * SEG], True, True,
               reads=[rROWR, rMISC])
            sop(lambda e: e.copy(SCRB[:], P[2][:]), reads=[rP[2]],
                writes=[rSCRB])
            vop(lambda e: e.tensor_tensor(AM[:], P[4][:], SCRB[:], OP.mult),
                reads=[rP[4], rSCRB], writes=[rAM])
            vop(lambda e, h=h: e.tensor_scalar(AM[:], AM[:], BET[:, h:h + 1],
                                               None, OP.mult),
                reads=[rAM, rMISC], writes=[rAM])
            emit_gather(2)

            for tc in range(4):
                bank = tc % 2
                mm(P[bank][:], rP[bank], KH[:, tc * 128:(tc + 1) * 128],
                   QH[:], True, False, reads=[rKHp, rQHp])
                mm(P[bank][:], rP[bank], ONER[:], ROWR[0:1, 0:SEG], False,
                   True, reads=[rROWR, rMISC])
                sop(lambda e, b=bank: e.activation(SCR1[:], P[b][:], AF.Exp,
                                                   bias=ZCOL[:], scale=SCALE),
                    reads=[rP[bank], rMISC], writes=[rSCR1])
                eb = tc % 2
                vop(lambda e, tc=tc, eb=eb: e.tensor_tensor(
                    EXPb[:, eb * SEG:(eb + 1) * SEG], SCR1[:],
                    MSK[:, tc * SEG:(tc + 1) * SEG], OP.mult),
                    reads=[rSCR1, rMISC], writes=[rEXP[eb]])
                emit_gather(2)
                mm(P[4][:], rP[4], VT[tc][:, h * 128:(h + 1) * 128],
                   EXPb[:, eb * SEG:(eb + 1) * SEG], tc == 0, tc == 3,
                   reads=[rVT[tc], rEXP[eb]])
                mm(P[2][0:2, :], rP[2], ONEC[:],
                   EXPb[:, eb * SEG:(eb + 1) * SEG], tc == 0, tc == 3,
                   reads=[rEXP[eb], rMISC])

            sop(lambda e: e.activation(ROWF[0:1, SEG:2 * SEG], P[2][0:1, :],
                                       AF.Ln, bias=ZCOL[0:1, :]),
                reads=[rP[2], rMISC], writes=[rROWF])
            sop(lambda e: e.activation(ROWR[0:1, 0:SEG],
                                       ROWF[0:1, SEG:2 * SEG], AF.Exp,
                                       bias=ZCOL[0:1, :], scale=-1.0),
                reads=[rROWF, rMISC], writes=[rROWR])
            mm(P[2][:], rP[2], ONER[:], ROWR[0:1, 0:SEG], True, True,
               reads=[rROWR, rMISC])
            sop(lambda e: e.copy(SCR1[:], P[2][:]), reads=[rP[2]],
                writes=[rSCR1])
            vop(lambda e: e.tensor_tensor(SCRB[:], P[4][:], SCR1[:], OP.mult),
                reads=[rP[4], rSCR1], writes=[rSCRB])
            emit_gather(2)
            scat = ATT[:].rearrange("p (c x) -> p x c", c=16)[
                :, h * 32:h * 32 + 32, :]
            vop(lambda e, h=h, scat=scat: e.scalar_tensor_tensor(
                scat, SCRB[:], OMB[:, h:h + 1],
                AM[:], OP.mult, OP.add),
                reads=[rSCRB, rAM, rMISC], writes=rATT)
            emit_gather(2)

            for tc in range(4):
                mm(P[3][:, 0:130], rP[3], SKt[:, tc * 128:(tc + 1) * 128],
                   MEMZ[:, h * 132:h * 132 + 130], True, True,
                   reads=[rSKp, rMEMZ[h]])
                vop(lambda e: e.reciprocal(RCP[:], P[3][:, 128:129]),
                    reads=[rP[3]], writes=[rRCP])
                vop(lambda e: e.tensor_scalar(SCR1[:, 0:128], P[3][:, 0:128],
                                              RCP[:], None, OP.mult),
                    reads=[rP[3], rRCP], writes=[rSCR1])
                vop(lambda e, tc=tc, h=h: e.tensor_tensor(
                    DLT[:, tc * 128:(tc + 1) * 128],
                    c32(VT[tc][:, h * 128:(h + 1) * 128]), SCR1[:, 0:128],
                    OP.subtract),
                    reads=[rVT[tc], rSCR1], writes=[rDLT, rEXP[0]])
                kb.ei("tensor", lambda e, tc=tc, SKt=SKt: e.transpose(
                    P[3][:, 256:384].bitcast(F32R),
                    SKt[:, tc * 128:(tc + 1) * 128], IDENT[:]),
                    reads=[rSKp, rMISC], writes=[rP[3]], sem=sT)
                sop(lambda e, tc=tc: e.copy(SKT[:, tc * 128:(tc + 1) * 128],
                                            P[3][:, 256:384]),
                    reads=[rP[3]], writes=[rSKT, rEXP[1]])
                emit_gather(2)
                mm(P[7][:, 0:128], rP[7], SKT[:, tc * 128:(tc + 1) * 128],
                   DLT[:, tc * 128:(tc + 1) * 128], tc == 0, tc == 3,
                   reads=[rSKT, rDLT, rEXP[0], rEXP[1]])
            vop(lambda e, h=h: e.tensor_tensor(
                MEMZ[:, h * 132:h * 132 + 128],
                c32(MEMZ[:, h * 132:h * 132 + 128]), P[7][:, 0:128], OP.add),
                reads=[rP[7]], writes=[rMEMZ[h]])
            vop(lambda e, SKt=SKt: e.reduce_sum(SCR1[:, 0:1], c32(SKt[:]),
                                                axis=AX.X),
                reads=[rSKp], writes=[rSCR1])
            vop(lambda e, h=h: e.tensor_tensor(
                MEMZ[:, h * 132 + 128:h * 132 + 129],
                c32(MEMZ[:, h * 132 + 128:h * 132 + 129]), SCR1[:, 0:1],
                OP.add), reads=[rSCR1], writes=[rMEMZ[h]])
            emit_gather(2)

        for blk in range(8):
            wt, wr = next_wbuf()
            wsrc = wo[:, blk * 256:(blk + 1) * 256].rearrange(
                "(c p) n -> p c n", p=128)
            wsem = sWA if wt is WAB[0] else sWB
            dma([(wt[:].rearrange("p (c n) -> p c n", n=256), wsrc)],
                writes=[wr], sem=wsem)
            for dcl in range(2):
                dc = blk * 2 + dcl
                bank = dcl
                for hc in range(16):
                    mm(P[bank][:], rP[bank],
                       wt[:, hc * 256 + dcl * 128:hc * 256 + (dcl + 1) * 128],
                       ATT[:, hc * SEG:(hc + 1) * SEG], hc == 0, hc == 15,
                       reads=[rATT[hc], wr])
                vop(lambda e, dc=dc, b=bank: e.tensor_tensor(
                    XF[:, dc * SEG:(dc + 1) * SEG],
                    c32(XF[:, dc * SEG:(dc + 1) * SEG]), P[b][:], OP.add),
                    reads=[rP[bank]], writes=[rXF[dc]])
                sop(lambda e, dc=dc: e.activation(
                    SQR[:], c32(XF[:, dc * SEG:(dc + 1) * SEG]),
                    AF.Square, bias=ZCOL[:]),
                    reads=[rXF[dc], rMISC], writes=[rSQR])
                mm(P[2][0:2, :], rP[2], ONEC[:], XF[:, dc * SEG:(dc + 1) * SEG],
                   dc == 0, dc == 15, reads=[rXF[dc], rMISC])
                mm(P[3][0:2, :], rP[3], ONEC[:], SQR[:], dc == 0, dc == 15,
                   reads=[rSQR, rMISC], force_sem=True)
        ln_and_store(sg, store_y=True)

    GROUPS = [(1, 4), (5, 8), (9, 12), (13, 15)]

    class GTile:
        __slots__ = ("r0", "nr", "tf", "dst", "rdst", "srcT", "ncols", "gi",
                     "slots", "stoks", "vtoks", "coll", "ai")

    _gt_state = {"gi": 0, "slot": 0}
    _prev_tile = [None, None, None, None]

    def tile_list():
        tiles = []
        arrays = [(wg_sh, rWGH[0], gidxT, D // 8, INTER, 0),
                  (wu_sh, rWGH[1], uidxT, D // 8, INTER, 1),
                  (wd_sh, rWGH[2], didxT, INTER // 8, D, 2)]
        for dst_sh, rdst, src_idxT, nrows, ncols, ai in arrays:
            r0 = 0
            atiles = []
            while r0 < nrows:
                nr = min(128, nrows - r0)
                for tf in range((ncols + GT - 1) // GT):
                    t = GTile()
                    t.r0, t.nr, t.tf = r0, nr, tf
                    t.dst, t.rdst, t.srcT, t.ncols = dst_sh, rdst, src_idxT, ncols
                    t.stoks, t.vtoks = [], []
                    t.coll = None
                    t.ai = ai
                    atiles.append(t)
                r0 += nr
            atiles[-1].coll = ai
            tiles.extend(atiles)
        return tiles

    sCC = kb.sem("sCC")

    def _emit_collective(ai):
        shard, full, rres = [
            (wg_shT, wg_hbmT, rWGH[0]),
            (wu_shT, wu_hbmT, rWGH[1]),
            (wd_shT, wd_hbmT, rWGH[2])][ai]
        toks = [rres.w] + rres.rs + [(sCC, sCC.n)]
        kb.waits("gpsimd", toks)

        def fn(e, shard=shard, full=full):
            return e.collective_compute(
                "AllGather", OP.bypass,
                replica_groups=[list(range(8))],
                ins=[shard.ap().opt()], outs=[full.ap().opt()])
        sCC.n += 1
        tok = (sCC, sCC.n)
        kb.q("gpsimd", lambda e, fn=fn, v=sCC.n: fn(e).then_inc(sCC.h, 1))
        rres.w = tok
        rres.rs = []

    if dbg:
        wgdump = nc.dram_tensor("wgdump", [D, INTER], F32R,
                                kind="ExternalOutput").ap()
        wddump = nc.dram_tensor("wddump", [INTER, D], F32R,
                                kind="ExternalOutput").ap()

    _all_tiles = tile_list()
    _tiles = _all_tiles
    _gtiles = []
    _cur = {"t": 0, "g": 0}
    sG = kb.sem("sG")


    def emit_group(t, gidx_):
        f0 = t.tf * GT
        fw = min(GT, t.ncols - f0)
        g0, g1 = GROUPS[gidx_]
        if gidx_ == 0:
            t.gi = _gt_state["gi"] % 2
            _gt_state["gi"] += 1
            t.slots = []
            dma([(GIX[t.gi][0:t.nr, 0:fw],
                  t.srcT[t.r0:t.r0 + t.nr, f0:f0 + fw])],
                writes=[rGIX[t.gi]], sem=sGXA if t.gi == 0 else sGXB)
        prev = _prev_tile[gidx_]
        ew = []
        if prev is not None and len(prev.vtoks) > gidx_:
            ew = [prev.vtoks[gidx_]]
        base = len(t.slots)
        for j in range(g0, g1 + 1):
            gs = _gt_state["slot"] % 15
            _gt_state["slot"] += 1
            t.slots.append(gs)
            sop(lambda e, j=j, gs=gs, fw=fw, gi=t.gi, nr=t.nr:
                e.activation(GS[gs][0:nr, 0:fw], GIX[gi][0:nr, 0:fw],
                             AF.Sign, bias=SGNB[0:nr, j - 1:j]),
                reads=[rGIX[t.gi], rMISC], writes=[rGS[gs]],
                sem=sS,
                extra_waits=ew if j == g0 else ())
        t.stoks.append((sS, sS.n))
        _prev_tile[gidx_] = t
        gact = OUTt if t.gi == 0 else GACB
        rgac = rGAC[t.gi]
        si = base
        for j in range(g0, g1 + 1):
            gs = t.slots[si]
            si += 1
            if j == 1:
                vop(lambda e, gs=gs, fw=fw, ga=gact, nr=t.nr:
                    e.tensor_scalar(ga[0:nr, 0:fw], GS[gs][0:nr, 0:fw],
                                    DIF[0:nr, 1:2], CC[0:nr, :],
                                    OP.mult, OP.add),
                    reads=[rGS[gs], rMISC], writes=[rgac],
                    sem=sV,
                    extra_waits=[t.stoks[-1]] if j == g0 else ())
            else:
                vop(lambda e, j=j, gs=gs, fw=fw, ga=gact, nr=t.nr:
                    e.scalar_tensor_tensor(
                        ga[0:nr, 0:fw], GS[gs][0:nr, 0:fw],
                        DIF[0:nr, j:j + 1],
                        ga[0:nr, 0:fw], OP.mult, OP.add),
                    reads=[rGS[gs], rMISC, rgac], writes=[rgac],
                    sem=sV,
                    extra_waits=[t.stoks[-1]] if j == g0 else ())
        t.vtoks.append((sV, sV.n))
        if gidx_ == len(GROUPS) - 1:
            dma([(t.dst[t.r0:t.r0 + t.nr, f0:f0 + fw],
                  gact[0:t.nr, 0:fw].bitcast(F32R))],
                reads=[rgac], writes=[t.rdst], sem=sWST[t.ai])
            if t.coll is not None:
                _emit_collective(t.coll)

    def emit_gather(n):
        for _ in range(n):
            ti = _cur["t"]
            if ti >= len(_tiles):
                return
            emit_group(_tiles[ti], _cur["g"])
            _cur["g"] += 1
            if _cur["g"] == len(GROUPS):
                _cur["g"] = 0
                _cur["t"] += 1

    def drain_gather():
        emit_gather(len(_tiles) * len(GROUPS))

    for sg in range(NSEG):
        seg_attention(sg)

    drain_gather()
    if dbg:
        kb.waits("gpsimd", [(sCC, sCC.n)])
        kb.q("gpsimd", lambda e: e.dma_start(out=wgdump, in_=wg_hbm)
             .then_inc(sDM.h, 16))
        kb.q("gpsimd", lambda e: e.dma_start(out=wddump, in_=wd_hbm)
             .then_inc(sDM.h, 16))
        sDM.n += 32

    HT = [(SQs[0], rSQ[0]), (SKs[0], rSK[0])]

    def ffn_chunk(ch):
        t0 = ch * SEG
        ysrc = y_hbm[:, t0:t0 + SEG].rearrange("(c p) s -> p c s", p=128)
        dma([(XF[:].rearrange("p (c s) -> p c s", s=SEG), ysrc)],
            reads=[rY], writes=rXF, sem=sXF)

        for oc in range(NOC):
            i = oc % 2
            wt, wr = WAB[i], rWAB[i]
            gsrc = wg_hbm[:, oc * 128:(oc + 1) * 128].rearrange(
                "(c p) n -> p c n", p=128)
            usrc = wu_hbm[:, oc * 128:(oc + 1) * 128].rearrange(
                "(c p) n -> p c n", p=128)
            wview = wt[:].rearrange("p (c n) -> p c n", n=256)
            wsem = sWA if wt is WAB[0] else sWB
            vtb = VT[oc % 4]
            rvtb = rVT[oc % 4]
            dma([(wview[:, :, 0:128], gsrc), (wview[:, :, 128:256], usrc)],
                reads=[rWGH[0], rWGH[1]], writes=[wr], sem=wsem)
            dma([(vtb[:, :], wd_hbm[oc * 128:(oc + 1) * 128, :])],
                reads=[rWGH[2]], writes=[rvtb], sem=sVT[oc % 4])

            pg, pu = P[0 + i], P[2 + i]
            for kc in range(16):
                mm(pg[:], rP[0 + i], wt[:, kc * 256:kc * 256 + 128],
                   XF[:, kc * SEG:(kc + 1) * SEG], kc == 0, kc == 15,
                   reads=[rXF[kc], wr])
                mm(pu[:], rP[2 + i], wt[:, kc * 256 + 128:(kc + 1) * 256],
                   XF[:, kc * SEG:(kc + 1) * SEG], kc == 0, kc == 15,
                   reads=[rXF[kc], wr])
            sop(lambda e, i=i: e.activation(SCR1[:], P[0 + i][:], AF.Silu,
                                            bias=ZCOL[:]),
                reads=[rP[0 + i], rMISC], writes=[rSCR1])
            ht, rht = HT[i]
            vop(lambda e, i=i, ht=ht: e.tensor_tensor(
                ht[:], SCR1[:], P[2 + i][:], OP.mult),
                reads=[rSCR1, rP[2 + i]], writes=[rht])

            if oc % 2 == 1 or oc == NOC - 1:
                ocs = [oc] if oc % 2 == 0 else [oc - 1, oc]
                for dc in range(16):
                    bank = 4 + dc % 2
                    for ii, oci in enumerate(ocs):
                        hti, rhti = HT[oci % 2]
                        mm(P[bank][:], rP[bank],
                           VT[oci % 4][:, dc * 128:(dc + 1) * 128], hti[:],
                           ii == 0, ii == len(ocs) - 1,
                           reads=[rVT[oci % 4], rhti])
                    if oc <= 1:
                        vop(lambda e, dc=dc, b=bank: e.tensor_copy(
                            ATT[:, dc * SEG:(dc + 1) * SEG], P[b][:]),
                            reads=[rP[bank]], writes=[rATT[dc]])
                    else:
                        vop(lambda e, dc=dc, b=bank: e.tensor_tensor(
                            ATT[:, dc * SEG:(dc + 1) * SEG],
                            c32(ATT[:, dc * SEG:(dc + 1) * SEG]), P[b][:],
                            OP.add), reads=[rP[bank]], writes=[rATT[dc]])

        for dc in range(16):
            vop(lambda e, dc=dc: e.tensor_tensor(
                XF[:, dc * SEG:(dc + 1) * SEG],
                c32(XF[:, dc * SEG:(dc + 1) * SEG]),
                c32(ATT[:, dc * SEG:(dc + 1) * SEG]), OP.add),
                reads=[rATT[dc]], writes=[rXF[dc]])
            sop(lambda e, dc=dc: e.activation(
                SQR[:], c32(XF[:, dc * SEG:(dc + 1) * SEG]), AF.Square,
                bias=ZCOL[:]), reads=[rXF[dc], rMISC], writes=[rSQR])
            mm(P[2][0:2, :], rP[2], ONEC[:], XF[:, dc * SEG:(dc + 1) * SEG],
               dc == 0, dc == 15, reads=[rXF[dc], rMISC])
            mm(P[3][0:2, :], rP[3], ONEC[:], SQR[:], dc == 0, dc == 15,
               reads=[rSQR, rMISC], force_sem=True)
        ln_and_store(ch, store_y=False)

    for ch in range(NSEG):
        ffn_chunk(ch)

    for _sm in [sDM, sXF, sWA, sWB, sGXA, sGXB, sYS, sOU] + sVT + sWST:
        if _sm.n:
            kb.q("sync", lambda e, s=_sm: e.wait_ge(s.h, s.n))
    kb.q("sync", lambda e: e.wait_ge(sCC.h, sCC.n))

    with ctx:
        with ExitStack() as semctx:
            for s in kb.sems.values():
                s.h = semctx.enter_context(nc.semaphore(s.name))
            with nc.Block() as block:
                kb.flush(block)
    return nc


def _host_prep(inputs):
    x = np.asarray(inputs["L0_attn_x"], dtype=np.float32)
    wq = np.ascontiguousarray(np.asarray(inputs["L0_attn_Wq"], np.float32))
    wk = np.ascontiguousarray(np.asarray(inputs["L0_attn_Wk"], np.float32))
    wv = np.ascontiguousarray(np.asarray(inputs["L0_attn_Wv"], np.float32))
    wo = np.ascontiguousarray(np.asarray(inputs["L0_attn_Wo"], np.float32))
    betas = np.asarray(inputs["L0_attn_betas"], np.float32)
    lut = np.asarray(inputs["L0_ffn_lut"], np.float32)
    gidx = np.asarray(inputs["L0_ffn_gate_idx"], np.int32)
    uidx = np.asarray(inputs["L0_ffn_up_idx"], np.int32)
    didx = np.asarray(inputs["L0_ffn_down_idx"], np.int32)

    memz0 = np.zeros((128, H * 132), dtype=np.float32)
    for h in range(H):
        memz0[:, h * 132 + 128] = 1.0 / 128.0

    maskT = np.zeros((128, 4 * SEG), dtype=np.float32)
    for tc in range(4):
        tl = np.arange(128)[:, None]
        s = np.arange(SEG)[None, :]
        maskT[:, tc * SEG:(tc + 1) * SEG] = (s >= (tc * 128 + tl)).astype(
            np.float32)

    shared = {
        "wq": wq, "wk": wk, "wv": wv, "wo": wo,

        "lutb": np.ascontiguousarray(np.broadcast_to(lut, (128, 16))),
        "betaTin": np.ascontiguousarray(betas[0, :, 0, :].T),
        "identin": np.eye(128, dtype=np.float32),
        "maskin": maskT.astype(ml_dtypes.bfloat16),
        "onecolin": np.ones((128, 2), np.float32),
        "onerowin": np.ones((1, 128), np.float32),
        "sgnin": np.ascontiguousarray(np.broadcast_to(
            (0.5 - np.arange(1, 16, dtype=np.float32))[None, :], (128, 15))),
        "memzin": memz0,
    }
    gidxT = np.ascontiguousarray(gidx.T)
    uidxT = np.ascontiguousarray(uidx.T)
    didxT = np.ascontiguousarray(didx.T)
    in_maps = []
    for b in range(8):
        m = dict(shared)
        m["xT"] = np.ascontiguousarray(x[b].T)
        m["gidxT"] = np.ascontiguousarray(gidxT[b * 256:(b + 1) * 256])
        m["uidxT"] = np.ascontiguousarray(uidxT[b * 256:(b + 1) * 256])
        m["didxT"] = np.ascontiguousarray(didxT[b * 688:(b + 1) * 688])
        in_maps.append(m)
    return in_maps


def kernel(**inputs):
    in_maps = _host_prep(inputs)
    nc = build_nc()
    res = run_bass_kernel_spmd(nc, in_maps, core_ids=list(range(8)))
    out = np.stack([np.ascontiguousarray(np.asarray(r["outT"]).T)
                    for r in res.results])
    return out.astype(np.float32)


if __name__ == "__main__":
    nc = build_nc()
    print("built ok")


# version 25
# speedup vs baseline: 1.0435x; 1.0372x over previous
import numpy as np
import ml_dtypes
from contextlib import ExitStack

import concourse.bass as bass
import concourse.mybir as mybir
from concourse.bass_utils import run_bass_kernel_spmd

F32 = mybir.dt.float32
F32R = mybir.dt.float32r
I32 = mybir.dt.int32
OP = mybir.AluOpType
AF = mybir.ActivationFunctionType
AX = mybir.AxisListType

D = 2048
S = 2048
SEG = 512
NSEG = 4
H = 16
INTER = 5504
NOC = INTER // 128
SCALE = 1.0 / float(np.sqrt(128.0))
EPS = 1e-5
GT = 512

ENGS = ("sync", "tensor", "vector", "scalar", "gpsimd")


class Sem:
    __slots__ = ("name", "h", "n")

    def __init__(self, name):
        self.name = name
        self.h = None
        self.n = 0


class Res:
    __slots__ = ("w", "rs")

    def __init__(self):
        self.w = None
        self.rs = []


class KB:
    def __init__(self, nc):
        self.nc = nc
        self.steps = []
        self.sems = {}
        self.waited = {e: {} for e in ENGS}
        self.pend = {e: [] for e in ENGS}

    def sem(self, name):
        if name not in self.sems:
            self.sems[name] = Sem(name)
        return self.sems[name]

    def q(self, eng, fn):
        self.steps.append((eng, fn))

    def wait(self, eng, tok):
        if tok is None:
            return
        sem, val = tok
        if val <= 0:
            return
        wd = self.waited[eng]
        if wd.get(sem.name, 0) >= val:
            return
        wd[sem.name] = val
        self.q(eng, lambda e, s=sem, v=val: e.wait_ge(s.h, v))

    def waits(self, eng, toks):
        best = {}
        for t in toks:
            if t is None:
                continue
            s, v = t
            if s.name not in best or best[s.name][1] < v:
                best[s.name] = t
        for t in best.values():
            self.wait(eng, t)

    def ei(self, eng, fn, reads=(), writes=(), sem=None, inc_n=1,
           extra_waits=()):
        toks = []
        for r in reads:
            toks.append(r.w)
        for r in writes:
            toks.append(r.w)
            toks.extend(r.rs)
        toks.extend(extra_waits)
        self.waits(eng, toks)
        tok = None
        if sem is not None:
            sem.n += inc_n
            tok = (sem, sem.n)

            def thunk(e, fn=fn, s=sem, n=inc_n):
                fn(e).then_inc(s.h, n)
            self.q(eng, thunk)
        else:
            self.q(eng, lambda e, fn=fn: fn(e))
        for r in writes:
            r.w = tok
            r.rs = []
        if tok is not None:
            for r in reads:
                r.rs.append(tok)
            for rds, wts in self.pend[eng]:
                for r in rds:
                    r.rs.append(tok)
                for r in wts:
                    if r.w is None:
                        r.w = tok
            self.pend[eng] = []
        else:
            self.pend[eng].append((list(reads), list(writes)))
        return tok

    def flush(self, block):
        for eng in ENGS:
            fns = [f for e, f in self.steps if e == eng]
            if not fns:
                continue

            def section(e, fns=fns):
                for f in fns:
                    f(e)
            getattr(block, eng)(section)


def build_nc(dbg=False):
    nc = bass.Bass("TRN2", target_bir_lowering=False)

    xT = nc.dram_tensor("xT", [D, S], F32R, kind="ExternalInput").ap()
    wq = nc.dram_tensor("wq", [D, D], F32R, kind="ExternalInput").ap()
    wk = nc.dram_tensor("wk", [D, D], F32R, kind="ExternalInput").ap()
    wv = nc.dram_tensor("wv", [D, D], F32R, kind="ExternalInput").ap()
    wo = nc.dram_tensor("wo", [D, D], F32R, kind="ExternalInput").ap()
    gidxT = nc.dram_tensor("gidxT", [D // 8, INTER], I32,
                           kind="ExternalInput").ap()
    uidxT = nc.dram_tensor("uidxT", [D // 8, INTER], I32,
                           kind="ExternalInput").ap()
    didxT = nc.dram_tensor("didxT", [INTER // 8, D], I32,
                           kind="ExternalInput").ap()
    lutb = nc.dram_tensor("lutb", [128, 16], F32, kind="ExternalInput").ap()
    betaTin = nc.dram_tensor("betaTin", [128, H], F32, kind="ExternalInput").ap()
    identin = nc.dram_tensor("identin", [128, 128], F32R,
                             kind="ExternalInput").ap()
    maskin = nc.dram_tensor("maskin", [128, 4 * SEG], mybir.dt.bfloat16,
                            kind="ExternalInput").ap()
    onecolin = nc.dram_tensor("onecolin", [128, 2], F32R,
                              kind="ExternalInput").ap()
    onerowin = nc.dram_tensor("onerowin", [1, 128], F32R,
                              kind="ExternalInput").ap()
    sgnin = nc.dram_tensor("sgnin", [128, 15], F32, kind="ExternalInput").ap()
    memzin = nc.dram_tensor("memzin", [128, H * 132], F32R,
                            kind="ExternalInput").ap()

    outT = nc.dram_tensor("outT", [D, S], F32, kind="ExternalOutput").ap()

    ikind = "ExternalOutput"
    y_hbm = nc.dram_tensor("y_hbm", [D, S], F32R, kind=ikind).ap()
    wg_hbmT = nc.dram_tensor("wg_hbm", [D, INTER], F32R, kind="Internal")
    wu_hbmT = nc.dram_tensor("wu_hbm", [D, INTER], F32R, kind="Internal")
    wd_hbmT = nc.dram_tensor("wd_hbm", [INTER, D], F32R, kind="Internal")
    wg_shT = nc.dram_tensor("wg_sh", [D // 8, INTER], F32R, kind="Internal")
    wu_shT = nc.dram_tensor("wu_sh", [D // 8, INTER], F32R, kind="Internal")
    wd_shT = nc.dram_tensor("wd_sh", [INTER // 8, D], F32R, kind="Internal")
    wg_hbm, wu_hbm, wd_hbm = wg_hbmT.ap(), wu_hbmT.ap(), wd_hbmT.ap()
    wg_sh, wu_sh, wd_sh = wg_shT.ap(), wu_shT.ap(), wd_shT.ap()

    ctx = ExitStack()

    def sb(name, shape, dt):
        return ctx.enter_context(nc.sbuf_tensor(name, shape, dt))

    def psum(name, shape):
        return ctx.enter_context(nc.psum_tensor(name, shape, F32))

    XF = sb("XF", [128, 16 * SEG], F32R)
    ATT = sb("ATT", [128, 16 * SEG], F32R)
    VT = [sb(f"VT{i}", [128, 2048], F32R) for i in range(4)]
    WAB = [sb("WA", [128, 4096], F32R), sb("WB", [128, 4096], F32R)]
    QHs = [sb("QH0", [128, SEG], F32R), sb("QH1", [128, SEG], F32R)]
    KHs = [sb("KH0", [128, SEG], F32R), sb("KH1", [128, SEG], F32R)]
    SQs = [sb("SQ0", [128, SEG], F32R), sb("SQ1", [128, SEG], F32R)]
    SKs = [sb("SK0", [128, SEG], F32R), sb("SK1", [128, SEG], F32R)]
    EXPb = sb("EXPb", [128, 2 * SEG], F32R)
    DLT = EXPb[:, 0:SEG]
    SKT = EXPb[:, SEG:2 * SEG]
    MEMZ = sb("MEMZ", [128, H * 132], F32R)
    SQR = sb("SQR", [128, SEG], F32R)
    SCR1s = [sb("SCR1a", [128, SEG], F32), sb("SCR1b", [128, SEG], F32)]
    SCRBs = [sb("SCRBa", [128, SEG], F32), sb("SCRBb", [128, SEG], F32)]
    AMs = [sb("AMa", [128, SEG], F32), sb("AMb", [128, SEG], F32)]
    SCR1 = SCR1s[0]
    SCRB = SCRBs[0]
    AM = AMs[0]
    OUTt = sb("OUTt", [128, SEG], F32)
    GIX = [sb("GIXA", [128, GT], I32), sb("GIXB", [128, GT], I32)]
    GS = [sb(f"GS{i}", [128, GT], mybir.dt.bfloat16) for i in range(15)]
    GACB = sb("GACB", [128, GT], F32)
    MSK = sb("MSK", [128, 4 * SEG], mybir.dt.bfloat16)
    IDENT = sb("IDENT", [128, 128], F32R)
    LUT = sb("LUT", [128, 16], F32)
    DIF = sb("DIF", [128, 16], F32)
    DIF2 = sb("DIF2", [128, 16], F32)
    CC = sb("CC", [128, 1], F32)
    BET = sb("BET", [128, H], F32)
    OMB = sb("OMB", [128, H], F32)
    BTI = sb("BTI", [128, H], F32)
    MKC = sb("MKC", [128, 1], F32)
    RCP = sb("RCP", [128, 1], F32)
    ZCOL = sb("ZCOL", [128, 1], F32)
    SGNB = sb("SGNB", [128, 15], F32)
    ONEC = sb("ONEC", [128, 2], F32R)
    ONER = sb("ONER", [1, 128], F32R)
    ROWR = sb("ROWR", [1, 4 * SEG], F32R)

    P = [psum(f"P{i}", [128, 512]) for i in range(8)]

    def c32(ap):
        return ap.bitcast(F32)

    kb = KB(nc)
    sDM = kb.sem("sDM")
    sXF = kb.sem("sXF")
    sWA = kb.sem("sWA")
    sWB = kb.sem("sWB")
    sVT = [kb.sem(f"sVT{i}") for i in range(4)]
    sGXA = kb.sem("sGXA")
    sGXB = kb.sem("sGXB")
    sWST = [kb.sem(f"sWST{i}") for i in range(3)]
    sYS = kb.sem("sYS")
    sOU = kb.sem("sOU")
    sT = kb.sem("sT")
    sV = kb.sem("sV")
    sS = kb.sem("sS")

    rXF = [Res() for _ in range(16)]
    rATT = [Res() for _ in range(16)]
    rVT = [Res() for _ in range(4)]
    rWAB = [Res(), Res()]
    rQH = [Res(), Res()]
    rKH = [Res(), Res()]
    rSQ = [Res(), Res()]
    rSK = [Res(), Res()]
    rSKT, rDLT = Res(), Res()
    rEXP = [Res(), Res()]
    rMEMZ = [Res() for _ in range(H)]
    rSQR, rOUT = Res(), Res()

    rGIX = [Res(), Res()]
    rSCR1s = [Res(), Res()]
    rSCRBs = [Res(), Res()]
    rAMs = [Res(), Res()]
    rSCR1 = rSCR1s[0]
    rSCRB = rSCRBs[0]
    rAM = rAMs[0]
    rGS = [Res() for _ in range(15)]
    rGAC = [rOUT, Res()]
    rP = [Res() for _ in range(8)]
    rROWR = Res()
    rMISC = Res()
    rRCP = Res()
    rY = Res()
    rWGH = [Res() for _ in range(3)]

    def dma(pairs, reads=(), writes=(), sem=None):
        sm = sem if sem is not None else sDM

        def fn(e, sm=sm):
            inst = None
            for dst, src in pairs:
                inst = e.dma_start(out=dst, in_=src)
                inst.then_inc(sm.h, 16)
            return inst

        toks = []
        for r in reads:
            toks.append(r.w)
        for r in writes:
            toks.append(r.w)
            toks.extend(r.rs)
        kb.waits("sync", toks)
        sm.n += 16 * len(pairs)
        tok = (sm, sm.n)
        kb.q("sync", fn)
        for r in writes:
            r.w = tok
            r.rs = []
        for r in reads:
            r.rs.append(tok)
        return tok

    sDG = kb.sem("sDG")

    def gdma(pairs, reads=(), writes=()):
        def fn(e):
            inst = None
            for dst, srcp in pairs:
                inst = e.dma_start(out=dst, in_=srcp)
                inst.then_inc(sDG.h, 16)
            return inst
        toks = []
        for r in reads:
            toks.append(r.w)
        for r in writes:
            toks.append(r.w)
            toks.extend(r.rs)
        kb.waits("gpsimd", toks)
        sDG.n += 16 * len(pairs)
        tok = (sDG, sDG.n)
        kb.q("gpsimd", fn)
        for r in writes:
            r.w = tok
            r.rs = []
        for r in reads:
            r.rs.append(tok)
        return tok

    def mm(bank_ap, rres, lhsT, rhs, start, stop, reads, force_sem=False):
        def fn(e, o=bank_ap, l=lhsT, r=rhs, st=start, sp=stop):
            return e.matmul(o, l, r, start=st, stop=sp)
        return kb.ei("tensor", fn, reads=list(reads), writes=[rres],
                     sem=sT if (stop or force_sem) else None)

    def vop(fn, reads=(), writes=(), sem=sV, extra_waits=()):
        return kb.ei("vector", fn, reads=list(reads), writes=list(writes),
                     sem=sem, extra_waits=extra_waits)

    def sop(fn, reads=(), writes=(), sem=sS, extra_waits=()):
        return kb.ei("scalar", fn, reads=list(reads), writes=list(writes),
                     sem=sem, extra_waits=extra_waits)

    vop(lambda e: e.memset(ZCOL[:], 0.0), writes=[rMISC])
    dma([(MEMZ[:], memzin), (MSK[:], maskin), (IDENT[:], identin),
         (LUT[:], lutb), (BTI[:], betaTin), (ONEC[:], onecolin),
         (ONER[:], onerowin), (SGNB[:], sgnin)],
        writes=list(rMEMZ) + [rMISC])

    sop(lambda e: e.activation(BET[:], BTI[:], AF.Sigmoid, bias=ZCOL[:]),
        reads=[rMISC], writes=[rMISC])
    vop(lambda e: e.tensor_scalar(OMB[:], BET[:], -1.0, 1.0, OP.mult, OP.add),
        reads=[rMISC], writes=[rMISC])
    vop(lambda e: e.tensor_tensor(DIF[:, 1:16], LUT[:, 1:16], LUT[:, 0:15],
                                  OP.subtract), reads=[rMISC], writes=[rMISC])
    vop(lambda e: e.tensor_scalar(DIF[:, 1:16], DIF[:, 1:16], 0.5, None,
                                  OP.mult), reads=[rMISC], writes=[rMISC])
    vop(lambda e: e.tensor_tensor(DIF2[:, 1:16], DIF[:, 1:16], DIF[:, 1:16],
                                  OP.add), reads=[rMISC], writes=[rMISC])
    vop(lambda e: e.reduce_sum(CC[:], DIF[:, 1:16], axis=AX.X),
        reads=[rMISC], writes=[rMISC])
    vop(lambda e: e.tensor_tensor(CC[:], CC[:], LUT[:, 0:1], OP.add),
        reads=[rMISC], writes=[rMISC])

    wab_i = [0]

    def next_wbuf():
        i = wab_i[0] % 2
        wab_i[0] += 1
        return WAB[i], rWAB[i]

    def ln_and_store(sg, store_y):
        MR = SCR1s[1][0:1, 0:SEG]
        VR = SCRBs[1][0:1, 0:SEG]
        rMR, rVR = rSCR1s[1], rSCRBs[1]
        vop(lambda e, MR=MR: e.tensor_scalar(MR, P[2][0:1, :],
                                             1.0 / 2048.0, None, OP.mult),
            reads=[rP[2]], writes=[rMR])
        vop(lambda e, VR=VR: e.tensor_scalar(VR, P[3][0:1, :],
                                             1.0 / 2048.0, None, OP.mult),
            reads=[rP[3]], writes=[rVR])
        vop(lambda e, MR=MR: e.tensor_tensor(SCR1[0:1, 0:SEG], MR, MR,
                                             OP.mult),
            reads=[rMR], writes=[rSCR1])
        vop(lambda e, VR=VR: e.tensor_tensor(VR, VR, SCR1[0:1, 0:SEG],
                                             OP.subtract),
            reads=[rVR, rSCR1], writes=[rVR])
        vop(lambda e, VR=VR: e.tensor_scalar(VR, VR, EPS, None, OP.add),
            reads=[rVR], writes=[rVR])
        sop(lambda e, VR=VR: e.activation(VR, VR, AF.Ln, bias=ZCOL[0:1, :]),
            reads=[rVR, rMISC], writes=[rVR])
        sop(lambda e, VR=VR: e.activation(ROWR[0:1, SEG:2 * SEG], VR, AF.Exp,
                                          bias=ZCOL[0:1, :], scale=-0.5),
            reads=[rVR, rMISC], writes=[rROWR])
        vop(lambda e, MR=MR: e.tensor_copy(ROWR[0:1, 0:SEG], MR),
            reads=[rMR], writes=[rROWR])
        mm(P[4][:], rP[4], ONER[:], ROWR[0:1, 0:SEG], True, True,
           reads=[rROWR, rMISC])
        mm(P[5][:], rP[5], ONER[:], ROWR[0:1, SEG:2 * SEG], True, True,
           reads=[rROWR, rMISC])
        sop(lambda e: e.copy(AM[:], P[4][:]), reads=[rP[4]], writes=[rAM])
        sop(lambda e: e.copy(SCRB[:], P[5][:]), reads=[rP[5]], writes=[rSCRB])
        vop(lambda e: e.tensor_tensor(AM[:], AM[:], SCRB[:], OP.mult),
            reads=[rSCRB, rAM], writes=[rAM])
        for dc in range(16):
            vop(lambda e, dc=dc: e.tensor_tensor(
                SCR1[:], c32(XF[:, dc * SEG:(dc + 1) * SEG]), SCRB[:],
                OP.mult),
                reads=[rXF[dc], rSCRB], writes=[rSCR1])
            if store_y:
                vop(lambda e, dc=dc: e.tensor_tensor(
                    XF[:, dc * SEG:(dc + 1) * SEG], SCR1[:], AM[:],
                    OP.subtract), reads=[rSCR1, rAM], writes=[rXF[dc]])
            else:
                vop(lambda e: e.tensor_tensor(OUTt[:], SCR1[:], AM[:],
                                              OP.subtract),
                    reads=[rSCR1, rAM], writes=[rOUT])
                dma([(outT[dc * 128:(dc + 1) * 128, sg * SEG:(sg + 1) * SEG],
                      OUTt[:])], reads=[rOUT], sem=sOU)
        if store_y:
            dst = y_hbm[:, sg * SEG:(sg + 1) * SEG].rearrange(
                "(c p) s -> p c s", p=128)
            dma([(dst, XF[:].rearrange("p (c s) -> p c s", s=SEG))],
                reads=rXF, writes=[rY], sem=sYS)

    def seg_attention(sg):
        t0 = sg * SEG
        xsrc = xT[:, t0:t0 + SEG].rearrange("(c p) s -> p c s", p=128)
        xdst = XF[:].rearrange("p (c s) -> p c s", s=SEG)
        dma([(xdst, xsrc)], writes=rXF, sem=sXF)

        for blk in range(8):
            wt, wr = next_wbuf()
            wsem = sWA if wt is WAB[0] else sWB
            wsrc = wv[:, blk * 256:(blk + 1) * 256].rearrange(
                "(c p) n -> p c n", p=128)
            dma([(wt[:].rearrange("p (c n) -> p c n", n=256), wsrc)],
                writes=[wr], sem=wsem)
            for tc in range(4):
                for kc in range(16):
                    mm(P[tc][:, 0:256], rP[tc],
                       XF[:, kc * SEG + tc * 128:kc * SEG + (tc + 1) * 128],
                       wt[:, kc * 256:(kc + 1) * 256], kc == 0, kc == 15,
                       reads=[rXF[kc], wr])
                sop(lambda e, tc=tc, blk=blk: e.copy(
                    VT[tc][:, blk * 256:(blk + 1) * 256], P[tc][:, 0:256]),
                    reads=[rP[tc]], writes=[rVT[tc]])

        def emit_prep(h):
            wt, wr = next_wbuf()
            qsrc = wq[:, h * 128:(h + 1) * 128].rearrange(
                "(c p) n -> p c n", p=128)
            ksrc = wk[:, h * 128:(h + 1) * 128].rearrange(
                "(c p) n -> p c n", p=128)
            wview = wt[:].rearrange("p (c n) -> p c n", n=256)
            wsem = sWA if wt is WAB[0] else sWB
            dma([(wview[:, :, 0:128], qsrc), (wview[:, :, 128:256], ksrc)],
                writes=[wr], sem=wsem)
            for kc in range(16):
                mm(P[5][:], rP[5], wt[:, kc * 256:kc * 256 + 128],
                   XF[:, kc * SEG:(kc + 1) * SEG], kc == 0, kc == 15,
                   reads=[rXF[kc], wr])
                mm(P[6][:], rP[6], wt[:, kc * 256 + 128:(kc + 1) * 256],
                   XF[:, kc * SEG:(kc + 1) * SEG], kc == 0, kc == 15,
                   reads=[rXF[kc], wr])
            p = h % 2
            sop(lambda e, p=p: e.copy(QHs[p][:], P[5][:]),
                reads=[rP[5]], writes=[rQH[p]])
            sop(lambda e, p=p: e.copy(KHs[p][:], P[6][:]),
                reads=[rP[6]], writes=[rKH[p]])

        def emit_bodyA(h):
            p = h % 2
            QH, KH, SQt, SKt = QHs[p], KHs[p], SQs[p], SKs[p]
            rQHp, rKHp, rSQp, rSKp = rQH[p], rKH[p], rSQ[p], rSK[p]
            SC1, SCB, AMp = SCR1s[p], SCRBs[p], AMs[p]
            rSC1, rSCB, rAMp = rSCR1s[p], rSCRBs[p], rAMs[p]
            NB = ROWR[0:1, p * SEG:(p + 1) * SEG]
            AUX = ROWR[0:1, (2 + p) * SEG:(3 + p) * SEG]

            def elu1(srcT, rsrc, dstT, rdst):
                vop(lambda e: e.tensor_scalar(SC1[:], c32(srcT[:]), 0.0, None,
                                              OP.min),
                    reads=[rsrc], writes=[rSC1])
                sop(lambda e: e.activation(SC1[:], SC1[:], AF.Exp,
                                           bias=ZCOL[:]),
                    reads=[rSC1, rMISC], writes=[rSC1])
                vop(lambda e: e.tensor_scalar(SCB[:], c32(srcT[:]), 0.0, None,
                                              OP.max),
                    reads=[rsrc], writes=[rSCB])
                vop(lambda e: e.tensor_tensor(dstT[:], SC1[:], SCB[:],
                                              OP.add),
                    reads=[rSC1, rSCB], writes=[rdst])

            elu1(QH, rQHp, SQt, rSQp)
            elu1(KH, rKHp, SKt, rSKp)
            emit_gather(1)

            vop(lambda e: e.reduce_max(MKC[:], c32(KH[:]), axis=AX.X,
                                       apply_absolute_value=True),
                reads=[rKHp], writes=[rMISC])
            vop(lambda e: e.scalar_tensor_tensor(
                SC1[:], c32(QH[:]), -1.0, c32(QH[:]), OP.mult, OP.max),
                reads=[rQHp], writes=[rSC1])
            vop(lambda e: e.tensor_scalar(SQR[:], SC1[:], MKC[:], None,
                                          OP.mult),
                reads=[rSC1, rMISC], writes=[rSQR])
            mm(P[6][0:2, :], rP[6], ONEC[:], SQR[:], True, True,
               reads=[rSQR, rMISC])
            vop(lambda e, NB=NB: e.tensor_scalar(
                NB, P[6][0:1, :], -1.0, None, OP.mult),
                reads=[rP[6]], writes=[rROWR])

            mm(P[5][:], rP[5], MEMZ[:, h * 132:h * 132 + 128], SQt[:], True,
               True, reads=[rMEMZ[h], rSQp])
            mm(P[6][0:2, :], rP[6], MEMZ[:, h * 132 + 128:h * 132 + 130],
               SQt[:], True, True, reads=[rMEMZ[h], rSQp, rROWR])
            sop(lambda e, SC1=SC1: e.activation(SC1[0:1, 0:SEG],
                                                P[6][0:1, :], AF.Ln,
                                                bias=ZCOL[0:1, :]),
                reads=[rP[6], rMISC], writes=[rSC1])
            sop(lambda e, AUX=AUX, SC1=SC1: e.activation(
                AUX, SC1[0:1, 0:SEG], AF.Exp, bias=ZCOL[0:1, :], scale=-1.0),
                reads=[rSC1, rMISC], writes=[rROWR])
            mm(P[6][:], rP[6], ONER[:], AUX, True, True,
               reads=[rROWR, rMISC])
            sop(lambda e: e.copy(SCB[:], P[6][:]), reads=[rP[6]],
                writes=[rSCB])
            vop(lambda e: e.tensor_tensor(AMp[:], P[5][:], SCB[:], OP.mult),
                reads=[rP[5], rSCB], writes=[rAMp])
            vop(lambda e, h=h: e.tensor_scalar(AMp[:], AMp[:],
                                               BET[:, h:h + 1], None,
                                               OP.mult),
                reads=[rAMp, rMISC], writes=[rAMp])
            emit_gather(1)

        def emit_bodyB(sg, h):
            p = h % 2
            QH, KH, SQt, SKt = QHs[p], KHs[p], SQs[p], SKs[p]
            rQHp, rKHp, rSQp, rSKp = rQH[p], rKH[p], rSQ[p], rSK[p]
            SC1, SCB, AMp = SCR1s[p], SCRBs[p], AMs[p]
            rSC1, rSCB, rAMp = rSCR1s[p], rSCRBs[p], rAMs[p]
            NB = ROWR[0:1, p * SEG:(p + 1) * SEG]
            AUX = ROWR[0:1, (2 + p) * SEG:(3 + p) * SEG]

            for tc in range(4):
                bank = tc % 2
                mm(P[bank][:], rP[bank], KH[:, tc * 128:(tc + 1) * 128],
                   QH[:], True, False, reads=[rKHp, rQHp])
                mm(P[bank][:], rP[bank], ONER[:], NB, False,
                   True, reads=[rROWR, rMISC])
                sop(lambda e, b=bank: e.activation(SC1[:], P[b][:], AF.Exp,
                                                   bias=ZCOL[:], scale=SCALE),
                    reads=[rP[bank], rMISC], writes=[rSC1])
                eb = tc % 2
                vop(lambda e, tc=tc, eb=eb: e.tensor_tensor(
                    EXPb[:, eb * SEG:(eb + 1) * SEG], SC1[:],
                    MSK[:, tc * SEG:(tc + 1) * SEG], OP.mult),
                    reads=[rSC1, rMISC], writes=[rEXP[eb]])
                mm(P[4][:], rP[4], VT[tc][:, h * 128:(h + 1) * 128],
                   EXPb[:, eb * SEG:(eb + 1) * SEG], tc == 0, tc == 3,
                   reads=[rVT[tc], rEXP[eb]])
                mm(P[2][0:2, :], rP[2], ONEC[:],
                   EXPb[:, eb * SEG:(eb + 1) * SEG], tc == 0, tc == 3,
                   reads=[rEXP[eb], rMISC])
                emit_gather(1)

            sop(lambda e, SC1=SC1: e.activation(SC1[0:1, 0:SEG],
                                                P[2][0:1, :], AF.Ln,
                                                bias=ZCOL[0:1, :]),
                reads=[rP[2], rMISC], writes=[rSC1])
            sop(lambda e, AUX=AUX, SC1=SC1: e.activation(
                AUX, SC1[0:1, 0:SEG], AF.Exp, bias=ZCOL[0:1, :], scale=-1.0),
                reads=[rSC1, rMISC], writes=[rROWR])
            mm(P[2][:], rP[2], ONER[:], AUX, True, True,
               reads=[rROWR, rMISC])
            sop(lambda e: e.copy(SC1[:], P[2][:]), reads=[rP[2]],
                writes=[rSC1])
            vop(lambda e: e.tensor_tensor(SCB[:], P[4][:], SC1[:], OP.mult),
                reads=[rP[4], rSC1], writes=[rSCB])
            scat = ATT[:].rearrange("p (c x) -> p x c", c=16)[
                :, h * 32:h * 32 + 32, :]
            vop(lambda e, h=h, scat=scat: e.scalar_tensor_tensor(
                scat, SCB[:], OMB[:, h:h + 1],
                AMp[:], OP.mult, OP.add),
                reads=[rSCB, rAMp, rMISC], writes=rATT)
            emit_gather(1)

            for tc in range(4):
                mm(P[3][:, 0:130], rP[3], SKt[:, tc * 128:(tc + 1) * 128],
                   MEMZ[:, h * 132:h * 132 + 130], True, True,
                   reads=[rSKp, rMEMZ[h]])
                vop(lambda e: e.reciprocal(RCP[:], P[3][:, 128:129]),
                    reads=[rP[3]], writes=[rRCP])
                vop(lambda e: e.tensor_scalar(SC1[:, 0:128], P[3][:, 0:128],
                                              RCP[:], None, OP.mult),
                    reads=[rP[3], rRCP], writes=[rSC1])
                vop(lambda e, tc=tc, h=h: e.tensor_tensor(
                    DLT[:, tc * 128:(tc + 1) * 128],
                    c32(VT[tc][:, h * 128:(h + 1) * 128]), SC1[:, 0:128],
                    OP.subtract),
                    reads=[rVT[tc], rSC1], writes=[rDLT, rEXP[0]])
                kb.ei("tensor", lambda e, tc=tc, SKt=SKt: e.transpose(
                    P[3][:, 256:384].bitcast(F32R),
                    SKt[:, tc * 128:(tc + 1) * 128], IDENT[:]),
                    reads=[rSKp, rMISC], writes=[rP[3]], sem=sT)
                sop(lambda e, tc=tc: e.copy(SKT[:, tc * 128:(tc + 1) * 128],
                                            P[3][:, 256:384]),
                    reads=[rP[3]], writes=[rSKT, rEXP[1]])
                mm(P[7][:, 0:128], rP[7], SKT[:, tc * 128:(tc + 1) * 128],
                   DLT[:, tc * 128:(tc + 1) * 128], tc == 0, tc == 3,
                   reads=[rSKT, rDLT, rEXP[0], rEXP[1]])
            vop(lambda e, h=h: e.tensor_tensor(
                MEMZ[:, h * 132:h * 132 + 128],
                c32(MEMZ[:, h * 132:h * 132 + 128]), P[7][:, 0:128], OP.add),
                reads=[rP[7]], writes=[rMEMZ[h]])
            vop(lambda e, SKt=SKt: e.reduce_sum(SC1[:, 0:1], c32(SKt[:]),
                                                axis=AX.X),
                reads=[rSKp], writes=[rSC1])
            vop(lambda e, h=h: e.tensor_tensor(
                MEMZ[:, h * 132 + 128:h * 132 + 129],
                c32(MEMZ[:, h * 132 + 128:h * 132 + 129]), SC1[:, 0:1],
                OP.add), reads=[rSC1], writes=[rMEMZ[h]])
            emit_gather(1)

        emit_prep(0)
        emit_bodyA(0)
        emit_prep(1)
        for h in range(H):
            if h + 1 < H:
                emit_bodyA(h + 1)
            emit_bodyB(sg, h)
            if h + 2 < H:
                emit_prep(h + 2)

        for blk in range(8):
            wt, wr = next_wbuf()
            wsrc = wo[:, blk * 256:(blk + 1) * 256].rearrange(
                "(c p) n -> p c n", p=128)
            wsem = sWA if wt is WAB[0] else sWB
            dma([(wt[:].rearrange("p (c n) -> p c n", n=256), wsrc)],
                writes=[wr], sem=wsem)
            for dcl in range(2):
                dc = blk * 2 + dcl
                bank = dcl
                for hc in range(16):
                    mm(P[bank][:], rP[bank],
                       wt[:, hc * 256 + dcl * 128:hc * 256 + (dcl + 1) * 128],
                       ATT[:, hc * SEG:(hc + 1) * SEG], hc == 0, hc == 15,
                       reads=[rATT[hc], wr])
                vop(lambda e, dc=dc, b=bank: e.tensor_tensor(
                    XF[:, dc * SEG:(dc + 1) * SEG],
                    c32(XF[:, dc * SEG:(dc + 1) * SEG]), P[b][:], OP.add),
                    reads=[rP[bank]], writes=[rXF[dc]])
                sop(lambda e, dc=dc: e.activation(
                    SQR[:], c32(XF[:, dc * SEG:(dc + 1) * SEG]),
                    AF.Square, bias=ZCOL[:]),
                    reads=[rXF[dc], rMISC], writes=[rSQR])
                mm(P[2][0:2, :], rP[2], ONEC[:], XF[:, dc * SEG:(dc + 1) * SEG],
                   dc == 0, dc == 15, reads=[rXF[dc], rMISC])
                mm(P[3][0:2, :], rP[3], ONEC[:], SQR[:], dc == 0, dc == 15,
                   reads=[rSQR, rMISC], force_sem=True)
        ln_and_store(sg, store_y=True)

    GROUPS = [(1, 4), (5, 8), (9, 12), (13, 15)]

    class GTile:
        __slots__ = ("r0", "nr", "tf", "dst", "rdst", "srcT", "ncols", "gi",
                     "slots", "stoks", "vtoks", "coll", "ai")

    _gt_state = {"gi": 0, "slot": 0}
    _prev_tile = [None, None, None, None]

    def tile_list():
        tiles = []
        arrays = [(wg_sh, rWGH[0], gidxT, D // 8, INTER, 0),
                  (wu_sh, rWGH[1], uidxT, D // 8, INTER, 1),
                  (wd_sh, rWGH[2], didxT, INTER // 8, D, 2)]
        for dst_sh, rdst, src_idxT, nrows, ncols, ai in arrays:
            r0 = 0
            atiles = []
            while r0 < nrows:
                nr = min(128, nrows - r0)
                for tf in range((ncols + GT - 1) // GT):
                    t = GTile()
                    t.r0, t.nr, t.tf = r0, nr, tf
                    t.dst, t.rdst, t.srcT, t.ncols = dst_sh, rdst, src_idxT, ncols
                    t.stoks, t.vtoks = [], []
                    t.coll = None
                    t.ai = ai
                    atiles.append(t)
                r0 += nr
            atiles[-1].coll = ai
            tiles.extend(atiles)
        return tiles

    sCC = kb.sem("sCC")

    def _emit_collective(ai):
        shard, full, rres = [
            (wg_shT, wg_hbmT, rWGH[0]),
            (wu_shT, wu_hbmT, rWGH[1]),
            (wd_shT, wd_hbmT, rWGH[2])][ai]
        toks = [rres.w] + rres.rs + [(sCC, sCC.n)]
        kb.waits("gpsimd", toks)

        def fn(e, shard=shard, full=full):
            return e.collective_compute(
                "AllGather", OP.bypass,
                replica_groups=[list(range(8))],
                ins=[shard.ap().opt()], outs=[full.ap().opt()])
        sCC.n += 1
        tok = (sCC, sCC.n)
        kb.q("gpsimd", lambda e, fn=fn, v=sCC.n: fn(e).then_inc(sCC.h, 1))
        rres.w = tok
        rres.rs = []

    if dbg:
        wgdump = nc.dram_tensor("wgdump", [D, INTER], F32R,
                                kind="ExternalOutput").ap()
        wddump = nc.dram_tensor("wddump", [INTER, D], F32R,
                                kind="ExternalOutput").ap()

    _all_tiles = tile_list()
    _tiles = _all_tiles
    _gtiles = []
    _cur = {"t": 0, "g": 0}
    sG = kb.sem("sG")


    def emit_group(t, gidx_):
        f0 = t.tf * GT
        fw = min(GT, t.ncols - f0)
        g0, g1 = GROUPS[gidx_]
        if gidx_ == 0:
            t.gi = _gt_state["gi"] % 2
            _gt_state["gi"] += 1
            t.slots = []
            dma([(GIX[t.gi][0:t.nr, 0:fw],
                  t.srcT[t.r0:t.r0 + t.nr, f0:f0 + fw])],
                writes=[rGIX[t.gi]], sem=sGXA if t.gi == 0 else sGXB)
        prev = _prev_tile[gidx_]
        ew = []
        if prev is not None and len(prev.vtoks) > gidx_:
            ew = [prev.vtoks[gidx_]]
        base = len(t.slots)
        for j in range(g0, g1 + 1):
            gs = _gt_state["slot"] % 15
            _gt_state["slot"] += 1
            t.slots.append(gs)
            sop(lambda e, j=j, gs=gs, fw=fw, gi=t.gi, nr=t.nr:
                e.activation(GS[gs][0:nr, 0:fw], GIX[gi][0:nr, 0:fw],
                             AF.Sign, bias=SGNB[0:nr, j - 1:j]),
                reads=[rGIX[t.gi], rMISC], writes=[rGS[gs]],
                sem=sS,
                extra_waits=ew if j == g0 else ())
        t.stoks.append((sS, sS.n))
        _prev_tile[gidx_] = t
        gact = OUTt if t.gi == 0 else GACB
        rgac = rGAC[t.gi]
        si = base
        for j in range(g0, g1 + 1):
            gs = t.slots[si]
            si += 1
            if j == 1:
                vop(lambda e, gs=gs, fw=fw, ga=gact, nr=t.nr:
                    e.tensor_scalar(ga[0:nr, 0:fw], GS[gs][0:nr, 0:fw],
                                    DIF[0:nr, 1:2], CC[0:nr, :],
                                    OP.mult, OP.add),
                    reads=[rGS[gs], rMISC], writes=[rgac],
                    sem=sV,
                    extra_waits=[t.stoks[-1]] if j == g0 else ())
            else:
                vop(lambda e, j=j, gs=gs, fw=fw, ga=gact, nr=t.nr:
                    e.scalar_tensor_tensor(
                        ga[0:nr, 0:fw], GS[gs][0:nr, 0:fw],
                        DIF[0:nr, j:j + 1],
                        ga[0:nr, 0:fw], OP.mult, OP.add),
                    reads=[rGS[gs], rMISC, rgac], writes=[rgac],
                    sem=sV,
                    extra_waits=[t.stoks[-1]] if j == g0 else ())
        t.vtoks.append((sV, sV.n))
        if gidx_ == len(GROUPS) - 1:
            dma([(t.dst[t.r0:t.r0 + t.nr, f0:f0 + fw],
                  gact[0:t.nr, 0:fw].bitcast(F32R))],
                reads=[rgac], writes=[t.rdst], sem=sWST[t.ai])
            if t.coll is not None:
                _emit_collective(t.coll)

    def emit_gather(n):
        for _ in range(n):
            ti = _cur["t"]
            if ti >= len(_tiles):
                return
            emit_group(_tiles[ti], _cur["g"])
            _cur["g"] += 1
            if _cur["g"] == len(GROUPS):
                _cur["g"] = 0
                _cur["t"] += 1

    def drain_gather():
        emit_gather(len(_tiles) * len(GROUPS))

    for sg in range(NSEG):
        seg_attention(sg)

    drain_gather()
    if dbg:
        kb.waits("gpsimd", [(sCC, sCC.n)])
        kb.q("gpsimd", lambda e: e.dma_start(out=wgdump, in_=wg_hbm)
             .then_inc(sDM.h, 16))
        kb.q("gpsimd", lambda e: e.dma_start(out=wddump, in_=wd_hbm)
             .then_inc(sDM.h, 16))
        sDM.n += 32

    HT = [(SQs[0], rSQ[0]), (SKs[0], rSK[0])]

    def ffn_chunk(ch):
        t0 = ch * SEG
        ysrc = y_hbm[:, t0:t0 + SEG].rearrange("(c p) s -> p c s", p=128)
        dma([(XF[:].rearrange("p (c s) -> p c s", s=SEG), ysrc)],
            reads=[rY], writes=rXF, sem=sXF)

        for oc in range(NOC):
            i = oc % 2
            wt, wr = WAB[i], rWAB[i]
            gsrc = wg_hbm[:, oc * 128:(oc + 1) * 128].rearrange(
                "(c p) n -> p c n", p=128)
            usrc = wu_hbm[:, oc * 128:(oc + 1) * 128].rearrange(
                "(c p) n -> p c n", p=128)
            wview = wt[:].rearrange("p (c n) -> p c n", n=256)
            wsem = sWA if wt is WAB[0] else sWB
            vtb = VT[oc % 4]
            rvtb = rVT[oc % 4]
            dma([(wview[:, :, 0:128], gsrc), (wview[:, :, 128:256], usrc)],
                reads=[rWGH[0], rWGH[1]], writes=[wr], sem=wsem)
            dma([(vtb[:, :], wd_hbm[oc * 128:(oc + 1) * 128, :])],
                reads=[rWGH[2]], writes=[rvtb], sem=sVT[oc % 4])

            pg, pu = P[0 + i], P[2 + i]
            for kc in range(16):
                mm(pg[:], rP[0 + i], wt[:, kc * 256:kc * 256 + 128],
                   XF[:, kc * SEG:(kc + 1) * SEG], kc == 0, kc == 15,
                   reads=[rXF[kc], wr])
                mm(pu[:], rP[2 + i], wt[:, kc * 256 + 128:(kc + 1) * 256],
                   XF[:, kc * SEG:(kc + 1) * SEG], kc == 0, kc == 15,
                   reads=[rXF[kc], wr])
            sop(lambda e, i=i: e.activation(SCR1[:], P[0 + i][:], AF.Silu,
                                            bias=ZCOL[:]),
                reads=[rP[0 + i], rMISC], writes=[rSCR1])
            ht, rht = HT[i]
            vop(lambda e, i=i, ht=ht: e.tensor_tensor(
                ht[:], SCR1[:], P[2 + i][:], OP.mult),
                reads=[rSCR1, rP[2 + i]], writes=[rht])

            if oc % 2 == 1 or oc == NOC - 1:
                ocs = [oc] if oc % 2 == 0 else [oc - 1, oc]
                for dc in range(16):
                    bank = 4 + dc % 2
                    for ii, oci in enumerate(ocs):
                        hti, rhti = HT[oci % 2]
                        mm(P[bank][:], rP[bank],
                           VT[oci % 4][:, dc * 128:(dc + 1) * 128], hti[:],
                           ii == 0, ii == len(ocs) - 1,
                           reads=[rVT[oci % 4], rhti])
                    if oc <= 1:
                        vop(lambda e, dc=dc, b=bank: e.tensor_copy(
                            ATT[:, dc * SEG:(dc + 1) * SEG], P[b][:]),
                            reads=[rP[bank]], writes=[rATT[dc]])
                    else:
                        vop(lambda e, dc=dc, b=bank: e.tensor_tensor(
                            ATT[:, dc * SEG:(dc + 1) * SEG],
                            c32(ATT[:, dc * SEG:(dc + 1) * SEG]), P[b][:],
                            OP.add), reads=[rP[bank]], writes=[rATT[dc]])

        for dc in range(16):
            vop(lambda e, dc=dc: e.tensor_tensor(
                XF[:, dc * SEG:(dc + 1) * SEG],
                c32(XF[:, dc * SEG:(dc + 1) * SEG]),
                c32(ATT[:, dc * SEG:(dc + 1) * SEG]), OP.add),
                reads=[rATT[dc]], writes=[rXF[dc]])
            sop(lambda e, dc=dc: e.activation(
                SQR[:], c32(XF[:, dc * SEG:(dc + 1) * SEG]), AF.Square,
                bias=ZCOL[:]), reads=[rXF[dc], rMISC], writes=[rSQR])
            mm(P[2][0:2, :], rP[2], ONEC[:], XF[:, dc * SEG:(dc + 1) * SEG],
               dc == 0, dc == 15, reads=[rXF[dc], rMISC])
            mm(P[3][0:2, :], rP[3], ONEC[:], SQR[:], dc == 0, dc == 15,
               reads=[rSQR, rMISC], force_sem=True)
        ln_and_store(ch, store_y=False)

    for ch in range(NSEG):
        ffn_chunk(ch)

    for _sm in [sDM, sXF, sWA, sWB, sGXA, sGXB, sYS, sOU] + sVT + sWST:
        if _sm.n:
            kb.q("sync", lambda e, s=_sm: e.wait_ge(s.h, s.n))
    kb.q("sync", lambda e: e.wait_ge(sCC.h, sCC.n))

    with ctx:
        with ExitStack() as semctx:
            for s in kb.sems.values():
                s.h = semctx.enter_context(nc.semaphore(s.name))
            with nc.Block() as block:
                kb.flush(block)
    return nc


def _host_prep(inputs):
    x = np.asarray(inputs["L0_attn_x"], dtype=np.float32)
    wq = np.ascontiguousarray(np.asarray(inputs["L0_attn_Wq"], np.float32))
    wk = np.ascontiguousarray(np.asarray(inputs["L0_attn_Wk"], np.float32))
    wv = np.ascontiguousarray(np.asarray(inputs["L0_attn_Wv"], np.float32))
    wo = np.ascontiguousarray(np.asarray(inputs["L0_attn_Wo"], np.float32))
    betas = np.asarray(inputs["L0_attn_betas"], np.float32)
    lut = np.asarray(inputs["L0_ffn_lut"], np.float32)
    gidx = np.asarray(inputs["L0_ffn_gate_idx"], np.int32)
    uidx = np.asarray(inputs["L0_ffn_up_idx"], np.int32)
    didx = np.asarray(inputs["L0_ffn_down_idx"], np.int32)

    memz0 = np.zeros((128, H * 132), dtype=np.float32)
    for h in range(H):
        memz0[:, h * 132 + 128] = 1.0 / 128.0

    maskT = np.zeros((128, 4 * SEG), dtype=np.float32)
    for tc in range(4):
        tl = np.arange(128)[:, None]
        s = np.arange(SEG)[None, :]
        maskT[:, tc * SEG:(tc + 1) * SEG] = (s >= (tc * 128 + tl)).astype(
            np.float32)

    shared = {
        "wq": wq, "wk": wk, "wv": wv, "wo": wo,

        "lutb": np.ascontiguousarray(np.broadcast_to(lut, (128, 16))),
        "betaTin": np.ascontiguousarray(betas[0, :, 0, :].T),
        "identin": np.eye(128, dtype=np.float32),
        "maskin": maskT.astype(ml_dtypes.bfloat16),
        "onecolin": np.ones((128, 2), np.float32),
        "onerowin": np.ones((1, 128), np.float32),
        "sgnin": np.ascontiguousarray(np.broadcast_to(
            (0.5 - np.arange(1, 16, dtype=np.float32))[None, :], (128, 15))),
        "memzin": memz0,
    }
    gidxT = np.ascontiguousarray(gidx.T)
    uidxT = np.ascontiguousarray(uidx.T)
    didxT = np.ascontiguousarray(didx.T)
    in_maps = []
    for b in range(8):
        m = dict(shared)
        m["xT"] = np.ascontiguousarray(x[b].T)
        m["gidxT"] = np.ascontiguousarray(gidxT[b * 256:(b + 1) * 256])
        m["uidxT"] = np.ascontiguousarray(uidxT[b * 256:(b + 1) * 256])
        m["didxT"] = np.ascontiguousarray(didxT[b * 688:(b + 1) * 688])
        in_maps.append(m)
    return in_maps


def kernel(**inputs):
    in_maps = _host_prep(inputs)
    nc = build_nc()
    res = run_bass_kernel_spmd(nc, in_maps, core_ids=list(range(8)))
    out = np.stack([np.ascontiguousarray(np.asarray(r["outT"]).T)
                    for r in res.results])
    return out.astype(np.float32)


if __name__ == "__main__":
    nc = build_nc()
    print("built ok")
